# version 19
# baseline (speedup 1.0000x reference)
import numpy as np
import ml_dtypes
from contextlib import ExitStack

import concourse.bass as bass
import concourse.mybir as mybir
from concourse.bass_utils import run_bass_kernel_spmd

F32 = mybir.dt.float32
BF16 = mybir.dt.bfloat16
AX = mybir.AxisListType.X
ALU = mybir.AluOpType
ACT = mybir.ActivationFunctionType

D = 1024
T = 2048
NCH = 8
NH = 16
E = 15
QB = 512
NQB = T // QB
TH = 1024
GW = 2432
EPS = 1e-5

DEBUG_DUMPS = False
PHASE_LIMIT = 5

ENGS = ["sync", "gpsimd", "tensor", "vector", "scalar"]
COMPUTE_OWNER = {"s_mm": "tensor", "s_ve": "vector", "s_sc": "scalar"}


class Em:
    def __init__(self, nc, ctx, semnames):
        self.nc = nc
        self.sems = {n: ctx.enter_context(nc.semaphore(n)) for n in semnames}
        self.count = {n: 0 for n in semnames}
        self.hw = {e: {} for e in ENGS}
        self.cur = None
        self.eng = None

    def now(self, sem):
        return self.count[sem]

    def act(self, eng, waits=()):
        on = self.cur == eng
        for (s, v) in waits:
            if v <= 0 or COMPUTE_OWNER.get(s) == eng:
                continue
            if on and v > self.hw[eng].get(s, 0):
                self.eng.wait_ge(self.sems[s], v)
                self.hw[eng][s] = v
        return on

    def fin(self, ins, sem, amt):
        self.count[sem] += amt
        if ins is not None:
            ins.then_inc(self.sems[sem], amt)
        return self.count[sem]

    def guard_all(self):
        for n in self.sems:
            v = self.count[n]
            if v <= 0 or COMPUTE_OWNER.get(n) == self.cur:
                continue
            if v > self.hw[self.cur].get(n, 0):
                self.eng.wait_ge(self.sems[n], v)
                self.hw[self.cur][n] = v

    def run_phase(self, name, fn):
        snap = dict(self.count)
        hw_snap = {e: dict(self.hw[e]) for e in ENGS}
        final = [None]
        with self.nc.Block(name) as blk:
            for engname in ENGS:
                self.count = dict(snap)
                self.hw[engname] = dict(hw_snap[engname])
                self.cur = engname

                def body(eng, _fn=fn):
                    self.eng = eng
                    _fn()
                    final[0] = dict(self.count)

                getattr(blk, engname)(body)
        self.count = final[0]
        self.cur = None


def build_program():
    nc = bass.Bass("TRN2", target_bir_lowering=False, debug=False)

    def din(name, shape, dtype=F32):
        return nc.dram_tensor(name, shape, dtype, kind="ExternalInput").ap()

    xT = din("xT", [D, T])
    wq = din("wq", [D, D]); wk = din("wk", [D, D])
    wv = din("wv", [D, D]); wo = din("wo", [D, D])
    sw1 = din("sw1", [D, D]); sw2 = din("sw2", [D, D])
    ew1 = din("ew1", [E, D, D]); ew2 = din("ew2", [E, D, D])
    rw = din("rw", [D, E])
    gH = din("g", [NH, 128, GW])
    bqc = din("bqc", [128, NCH]); bkc = din("bkc", [128, NCH])
    bvc = din("bvc", [128, NCH]); boc = din("boc", [128, NCH])
    sb1c = din("sb1c", [128, NCH]); sb2c = din("sb2c", [128, NCH])
    eb1r = din("eb1r", [128, E * NCH]); eb2n = din("eb2n", [E, D])
    rbrep = din("rbrep", [128, E])
    onesb = din("onesb", [128, 1], BF16)
    indb = din("indb", [128, 2], BF16)
    identb = din("identb", [128, 128], BF16)
    id64db = din("id64db", [128, 64], BF16)

    out_h = nc.dram_tensor("out", [NCH, 128, T], F32, kind="ExternalOutput").ap()
    _dk = "ExternalOutput" if DEBUG_DUMPS else "Internal"
    x2s = nc.dram_tensor("x2s", [NCH, 128, T], F32, kind=_dk).ap()
    bscr = nc.dram_tensor("bscr", [4, T], F32, kind="Internal").ap()
    bscrb = nc.dram_tensor("bscrb", [4, T], BF16, kind="Internal").ap()
    oOs = nc.dram_tensor("oOs", [NCH, 128, T], BF16, kind=_dk).ap()

    dumps = {}
    if DEBUG_DUMPS:
        for name, shape in [
            ("d_ln1", [128, NCH * T]), ("d_q", [128, T]), ("d_k", [128, T]),
            ("d_va", [128, 16 * 130]), ("d_pt0", [128, 16 * QB]),
            ("d_attno", [128, NCH * T]), ("d_ln2", [128, NCH * T]),
            ("d_cw", [16, T]), ("d_negc", [128, 2]),
            ("d_m3", [1, T]), ("d_r3", [1, T]), ("d_m3b", [1, T]), ("d_r3b", [1, T]),
            ("d_R0", [1, T]), ("d_R1", [1, T]), ("d_R2", [1, T]),
            ("d_cwp", [128, 128]), ("d_lg", [128, 15]), ("d_t8", [128, 8]),
            ("d_mr1", [128, T]), ("d_rr1", [128, T]),
        ]:
            dumps[name] = nc.dram_tensor(name, shape, F32, kind="ExternalOutput").ap()

    semnames = (
        ["s_mm", "s_ve", "s_sc"]
        + ["s_dinit", "s_dw0", "s_dw1", "s_dw2"]
        + ["s_dg0", "s_dg1", "s_dg2", "s_dtb0", "s_dtb1"]
        + ["s_db0", "s_db1", "s_db2", "s_db3", "s_dbw"]
        + ["s_dxr0", "s_dxr1", "s_dor0", "s_dor1", "s_dx2w", "s_doow",
           "s_dst", "s_dcw0", "s_dcw1", "s_dxc0", "s_dxc1"]
    )

    with ExitStack() as ctx:
        em = Em(nc, ctx, semnames)
        alloc = ctx.enter_context

        sb_bq = alloc(nc.sbuf_tensor("sb_bq", [128, NCH], F32))
        sb_bk = alloc(nc.sbuf_tensor("sb_bk", [128, NCH], F32))
        sb_bv = alloc(nc.sbuf_tensor("sb_bv", [128, NCH], F32))
        sb_bo = alloc(nc.sbuf_tensor("sb_bo", [128, NCH], F32))
        sb_sb1 = alloc(nc.sbuf_tensor("sb_sb1", [128, NCH], F32))
        sb_sb2 = alloc(nc.sbuf_tensor("sb_sb2", [128, NCH], F32))
        sb_eb1 = alloc(nc.sbuf_tensor("sb_eb1", [128, E * NCH], F32))
        sb_rb = alloc(nc.sbuf_tensor("sb_rb", [128, E], F32))
        sb_eb2 = alloc(nc.sbuf_tensor("sb_eb2", [E, D], BF16))
        sb_rw = alloc(nc.sbuf_tensor("sb_rw", [128, NCH * E], BF16))
        sb_ones = alloc(nc.sbuf_tensor("sb_ones", [128, 1], BF16))
        sb_ind = alloc(nc.sbuf_tensor("sb_ind", [128, 2], BF16))
        sb_id = alloc(nc.sbuf_tensor("sb_id", [128, 128], BF16))
        sb_id64 = alloc(nc.sbuf_tensor("sb_id64", [128, 64], BF16))
        sb_bq8 = alloc(nc.sbuf_tensor("sb_bq8", [128, NCH], F32))
        sb_negc = alloc(nc.sbuf_tensor("sb_negc", [128, 2], F32))
        sb_cw = alloc(nc.sbuf_tensor("sb_cw", [16, T], BF16))
        sb_qm = alloc(nc.sbuf_tensor("sb_qm", [2, 16], F32))
        sb_eps = alloc(nc.sbuf_tensor("sb_eps", [1, 1], F32))

        PSA = alloc(nc.psum_tensor("psa", [128, 1024], F32))
        PSB = alloc(nc.psum_tensor("psb", [128, 1024], F32))
        PSW0 = alloc(nc.psum_tensor("psw0", [128, 1024], F32))
        PSW1 = alloc(nc.psum_tensor("psw1", [128, 1024], F32))
        PS0 = PSA[:, 0:512]
        PS1 = PSA[:, 512:1024]
        PS6 = PSB[:, 0:512]
        PS7 = PSB[:, 512:1024]
        PS = [PS0, PS1, None, None, None, None, PS6, PS7]
        PSW = [PSW0, PSW1]
        SUMAP = [PSW0[0:1, 0:512], PSW0[0:1, 512:1024],
                 PSW1[0:1, 0:512], PSW1[0:1, 512:1024]]
        SQAP = [PS0[0:1, :], PS1[0:1, :], PS6[0:1, :], PS7[0:1, :]]
        W2P = [PSW0[:, 0:512], PSW0[:, 512:1024]]
        PTB = [PSW1[:].bitcast(BF16)[:, 0:1024], PSW1[:].bitcast(BF16)[:, 1024:2048]]

        WB = [alloc(nc.sbuf_tensor(f"sb_w{i}", [128, NCH * 1024], BF16)) for i in range(3)]
        sb_lnx = alloc(nc.sbuf_tensor("sb_lnx", [128, NCH * T], BF16))

        def wchunk(ap):
            return ap.rearrange("(c p) o -> c p o", p=128).transpose([1, 0, 2])

        def sb3(t, inner):
            return t[:].rearrange("p (c o) -> p c o", o=inner)
        def dma(queue, out, in_, sem, waits=()):
            ins = None
            if em.act(queue, waits):
                ins = em.eng.dma_start(out=out, in_=in_)
            return em.fin(ins, sem, 16)

        BROW = {"s_db0": 0, "s_db1": 1, "s_db2": 2, "s_db3": 3}

        def bcast(row_ap, out_ap, sem, waits=(), queue="sync"):
            n_part = out_ap.shape[0]
            n = row_ap.shape[-1]
            row = BROW[sem]
            scr = bscr if row_ap.dtype == F32 else bscrb
            ins = None
            if em.act(queue, list(waits) + [(sem, em.now(sem))]):
                ins = em.eng.dma_start(out=scr[row:row + 1, 0:n], in_=row_ap)
            em.fin(ins, "s_dbw", 16)
            ins = None
            if em.act(queue, [("s_dbw", em.now("s_dbw"))]):
                ins = em.eng.dma_start(
                    out=out_ap, in_=scr[row:row + 1, 0:n].partition_broadcast(n_part))
            return em.fin(ins, sem, 16)

        def mme(out, lhsT, rhs, start, stop, waits=(), tp=None, inc=False):
            ins = None
            if em.act("tensor", waits):
                ins = em.eng.matmul(out, lhsT, rhs, start=start, stop=stop,
                                    tile_position=tp, skip_group_check=True)
            if inc:
                return em.fin(ins, "s_mm", 1)
            return None

        def trans(out, in_, waits=()):
            ins = None
            if em.act("tensor", waits):
                ins = em.eng.transpose(out, in_, sb_id[:])
            return em.fin(ins, "s_mm", 1)

        def ve(f, waits=()):
            ins = None
            if em.act("vector", waits):
                ins = f(em.eng)
            return em.fin(ins, "s_ve", 1)

        def sc(f, waits=()):
            ins = None
            if em.act("scalar", waits):
                ins = f(em.eng)
            return em.fin(ins, "s_sc", 1)

        def ln_rows(R0, R1, R2, n_tok, sum_srcs, sq_srcs, mrep, rrep, bc0, bc1):
            wl = [("s_mm", em.now("s_mm"))]
            for (ap_, c0, w) in sum_srcs:
                ve(lambda e, a=ap_, c=c0, ww=w: e.tensor_copy(R0[0:1, c:c + ww], a), waits=wl)
            for (ap_, c0, w) in sq_srcs:
                ve(lambda e, a=ap_, c=c0, ww=w: e.tensor_copy(R1[0:1, c:c + ww], a))
            ve(lambda e: e.tensor_scalar_mul(R0[0:1, :n_tok], R0[0:1, :n_tok], 1.0 / D))
            ve(lambda e: e.tensor_mul(R2[0:1, :n_tok], R0[0:1, :n_tok], R0[0:1, :n_tok]))
            ve(lambda e: e.scalar_tensor_tensor(R1[0:1, :n_tok], R1[0:1, :n_tok], 1.0 / D,
                                                R2[0:1, :n_tok], ALU.mult, ALU.subtract))
            sc(lambda e: e.activation(out=R1[0:1, :n_tok], in_=R1[0:1, :n_tok],
                                      func=ACT.Ln, bias=sb_eps[0:1, 0:1]),
               waits=[("s_ve", em.now("s_ve")), ("s_ve", eps_ve[0])])
            sc(lambda e: e.activation(out=R2[0:1, :n_tok], in_=R1[0:1, :n_tok],
                                      func=ACT.Exp, scale=-0.5))
            vnow = em.now("s_ve")
            snow = em.now("s_sc")
            b0 = bcast(R0[0:1, :n_tok], mrep[:, :n_tok], bc0, waits=[("s_ve", vnow)])
            b1 = bcast(R2[0:1, :n_tok], rrep[:, :n_tok], bc1, waits=[("s_sc", snow)],
                       queue="gpsimd")
            return b0, b1

        ph1 = ExitStack()
        sb_x1 = ph1.enter_context(nc.sbuf_tensor("sb_x1", [128, NCH * T], F32))
        sb_xbc = [ph1.enter_context(nc.sbuf_tensor("sb_xbc0", [128, T], BF16))] * 2
        sb_sqc = [ph1.enter_context(nc.sbuf_tensor("sb_sqc0", [128, T], BF16))] * 2
        sb_R0 = ph1.enter_context(nc.sbuf_tensor("sb_R0", [1, T], F32))
        sb_R1 = ph1.enter_context(nc.sbuf_tensor("sb_R1", [1, T], F32))
        sb_R2 = ph1.enter_context(nc.sbuf_tensor("sb_R2", [1, T], F32))
        sb_mrep = ph1.enter_context(nc.sbuf_tensor("sb_mrep", [128, T], F32))
        sb_rrep = ph1.enter_context(nc.sbuf_tensor("sb_rrep", [128, T], F32))

        xTc = xT.rearrange("(c p) t -> c p t", p=128)

        eps_ve = [0]

        def phase1():
            eps_ve[0] = ve(lambda e: e.memset(sb_eps[:], EPS))
            for (dst, src) in [(sb_bq, bqc), (sb_bk, bkc), (sb_bv, bvc), (sb_bo, boc),
                               (sb_sb1, sb1c), (sb_sb2, sb2c), (sb_eb1, eb1r),
                               (sb_rb, rbrep), (sb_ones, onesb), (sb_ind, indb),
                               (sb_id, identb), (sb_id64, id64db)]:
                dma("sync", dst[:], src, "s_dinit")
            dma("gpsimd", sb3(sb_rw, E), wchunk(rw), "s_dcw0")
            dma("gpsimd", sb_eb2[:], eb2n, "s_dcw1")
            dma("sync", sb_x1[:].rearrange("p (c t) -> p c t", t=T),
                xTc.transpose([1, 0, 2]), "s_dxc0")
            mm_after = {}
            for c in range(NCH):
                sl = c % 2
                sc(lambda e, s=sl, cc=c: e.copy(sb_xbc[s][:], sb_x1[:, cc * T:(cc + 1) * T]),
                   waits=[("s_dxc0", em.now("s_dxc0")), ("s_mm", mm_after.get(c - 1, 0))])
                sc(lambda e, s=sl: e.activation(out=sb_sqc[s][:], in_=sb_xbc[s][:],
                                                func=ACT.Square))
                scn = em.now("s_sc")
                for qc in range(NQB):
                    mme(SUMAP[qc], sb_ones[:], sb_xbc[sl][:, qc * 512:(qc + 1) * 512],
                        start=(c == 0), stop=(c == NCH - 1),
                        waits=[("s_sc", scn)], inc=(c == NCH - 1))
                for qc in range(NQB):
                    mme(SQAP[qc], sb_ones[:], sb_sqc[sl][:, qc * 512:(qc + 1) * 512],
                        start=(c == 0), stop=(c == NCH - 1),
                        inc=(c == NCH - 1 or qc == NQB - 1))
                mm_after[c] = em.now("s_mm")
            ln_rows(sb_R0, sb_R1, sb_R2, T,
                    [(SUMAP[qc], qc * 512, 512) for qc in range(NQB)],
                    [(SQAP[qc], qc * 512, 512) for qc in range(NQB)],
                    sb_mrep, sb_rrep, "s_db0", "s_db1")
            for c in range(NCH):
                wl = [("s_db0", em.now("s_db0")), ("s_db1", em.now("s_db1"))]
                ve(lambda e, cc=c: e.tensor_sub(sb_x1[:, cc * T:(cc + 1) * T],
                                                sb_x1[:, cc * T:(cc + 1) * T],
                                                sb_mrep[:]), waits=wl)
                ve(lambda e, cc=c: e.tensor_mul(sb_lnx[:, cc * T:(cc + 1) * T],
                                                sb_x1[:, cc * T:(cc + 1) * T], sb_rrep[:]))
            if DEBUG_DUMPS:
                wl = [("s_ve", em.now("s_ve")), ("s_sc", em.now("s_sc")),
                      ("s_db0", em.now("s_db0")), ("s_db1", em.now("s_db1"))]
                dma("gpsimd", dumps["d_ln1"], sb_lnx[:], "s_dst", waits=wl)
                dma("sync", dumps["d_R0"], sb_R0[0:1, :], "s_dst", waits=wl)
                dma("sync", dumps["d_R1"], sb_R1[0:1, :], "s_dst")
                dma("sync", dumps["d_R2"], sb_R2[0:1, :], "s_dst")
                dma("sync", dumps["d_mr1"], sb_mrep[:], "s_dst")
                dma("sync", dumps["d_rr1"], sb_rrep[:], "s_dst")

        em.run_phase("ln1", phase1)
        ph1.close()

        phA = ExitStack()
        sb_ao = phA.enter_context(nc.sbuf_tensor("sb_ao", [128, NCH * T], BF16))
        ph2 = ExitStack()
        sb_g = [ph2.enter_context(nc.sbuf_tensor(f"sb_g{i}", [128, 2 * GW], BF16)) for i in range(2)]
        sb_q = ph2.enter_context(nc.sbuf_tensor("sb_q", [128, T], BF16))
        sb_k = ph2.enter_context(nc.sbuf_tensor("sb_k", [128, T], BF16))
        sb_vf = ph2.enter_context(nc.sbuf_tensor("sb_vf", [128, T], BF16))
        sb_va = ph2.enter_context(nc.sbuf_tensor("sb_va", [128, 16 * 130], BF16))
        sb_pt2 = ph2.enter_context(nc.sbuf_tensor("sb_pt2", [128, 16 * 2 * QB], BF16))
        sb_rr = ph2.enter_context(nc.sbuf_tensor("sb_rr", [1, 2 * QB], F32))
        sb_cc = ph2.enter_context(nc.sbuf_tensor("sb_cc", [2, 16], F32))
        AVBANK = {0: 0, 1: 1, 2: 0, 3: 0}


        def attention():
            em.guard_all()
            st = {
                "pair_mm": em.now("s_mm"),
                "pair_ve": em.now("s_ve"),
                "pair_sc": em.now("s_sc"),
                "g_rdy": {},
                "g_war": {},
                "qb_norm_ve": {},
                "av_done_mm": 0,
                "tt_done": {},
            }
            for i, w in enumerate([wq, wk, wv]):
                dma("gpsimd", sb3(WB[i], 1024), wchunk(w), f"s_dw{i}")
            sc(lambda e: e.mul(WB[0][:], WB[0][:], 0.125), waits=[("s_dw0", 16)])
            sc(lambda e: e.mul(sb_bq8[:], sb_bq[:], 0.125),
               waits=[("s_dinit", em.now("s_dinit"))])
            st["wqs"] = em.now("s_sc")
            va3 = sb_va[:].rearrange("p (t c) -> p t c", c=130)
            ve(lambda e: e.memset(va3[:, :, 64:65], 1.0))
            ve(lambda e: e.memset(va3[:, :, 129:130], 1.0))
            st["va_ones"] = em.now("s_ve")
            ve(lambda e: e.memset(sb_e1[0:1, :], 1.0))
            st["e1_ve"] = em.now("s_ve")
            st["g_dma"] = {}

            def load_g(p_, war_ve):
                slot = p_ % 2
                dma("gpsimd", sb_g[slot][:].rearrange("p (h w) -> p h w", w=GW),
                    gH[2 * p_: 2 * p_ + 2].transpose([1, 0, 2]), f"s_dg{slot}",
                    waits=[("s_ve", war_ve)])
                st["g_dma"][slot] = em.now(f"s_dg{slot}")

            def exp_g(p_):
                slot = p_ % 2
                sc(lambda e, sl=slot: e.activation(out=sb_g[sl][:], in_=sb_g[sl][:],
                                                   func=ACT.Exp),
                   waits=[(f"s_dg{slot}", st["g_dma"][slot])])
                st["g_rdy"][slot] = em.now("s_sc")

            load_g(0, 0)
            exp_g(0)
            st["load_g"] = load_g
            st["exp_g"] = exp_g
            for p in range(NCH):
                pair_prologue(p, st)
                av_prev = None
                for qb in range(NQB):
                    av_prev = qb_scores(p, qb, st, av_prev)
                qb_av_tail(p, NQB - 1, st)
                ve(lambda e, pp=p: e.tensor_scalar_add(
                    sb_ao[:, pp * T:(pp + 1) * T], sb_ao[:, pp * T:(pp + 1) * T],
                    sb_bv[:, pp:pp + 1]))
                st["pair_mm"] = em.now("s_mm")
                st["pair_ve"] = em.now("s_ve")
                st["pair_sc"] = em.now("s_sc")
                st["g_war"][p % 2] = em.now("s_ve")
            if DEBUG_DUMPS:
                wl = [("s_sc", em.now("s_sc")), ("s_ve", em.now("s_ve"))]
                dma("gpsimd", dumps["d_attno"], sb_ao[:], "s_dst", waits=wl)
                dma("gpsimd", dumps["d_q"], sb_q[:], "s_dst", waits=wl)
                dma("gpsimd", dumps["d_k"], sb_k[:], "s_dst", waits=wl)
                dma("gpsimd", dumps["d_va"], sb_va[:], "s_dst", waits=wl)
                dma("gpsimd", dumps["d_pt0"], sb_pt2[:, 0:16 * QB], "s_dst", waits=wl)
                dma("sync", dumps["d_negc"], sb_negc[:], "s_dst", waits=wl)

        def pair_prologue(p, st):
            pv, pm, psc = st["pair_ve"], st["pair_mm"], st["pair_sc"]
            ev = {}
            for wi, (wsem, bias, dest) in enumerate([("s_dw0", sb_bq8, sb_q),
                                                     ("s_dw1", sb_bk, sb_k)]):
                for qb in range(NQB):
                    ps = PS[6] if qb % 2 == 0 else PS[7]
                    wl = [(wsem, 16), ("s_ve", st["qb_norm_ve"].get(1, 0)),
                          ("s_sc", st["wqs"])]
                    key = (wi, qb)
                    if (wi, qb) not in [(0, 0), (0, 1)]:
                        pk = (wi, qb - 2) if qb >= 2 else (wi - 1, qb + 2)
                        if pk in ev:
                            wl.append(("s_ve", ev[pk]))
                    for dc in range(NCH):
                        mme(ps[:], WB[wi][:, dc * 1024 + p * 128: dc * 1024 + p * 128 + 128],
                            sb_lnx[:, dc * T + qb * 512: dc * T + qb * 512 + 512],
                            start=(dc == 0), stop=(dc == NCH - 1),
                            waits=wl if dc == 0 else (), inc=(dc == NCH - 1))
                    ev[key] = ve(lambda e, pp=ps, d=dest, q=qb, b=bias:
                                 e.tensor_scalar_add(d[:, q * 512:(q + 1) * 512],
                                                     pp[:], b[:, p:p + 1]),
                                 waits=[("s_mm", em.now("s_mm")), ("s_mm", pm)])
            evv = {}
            for qb in range(NQB):
                ps = PS[6] if qb % 2 == 0 else PS[7]
                wl = [("s_dw2", 16)]
                if qb >= 2:
                    wl.append(("s_ve", evv[qb - 2]))
                else:
                    wl.append(("s_ve", ev[(1, qb + 2)]))
                for dc in range(NCH):
                    mme(ps[:], WB[2][:, dc * 1024 + p * 128: dc * 1024 + p * 128 + 128],
                        sb_lnx[:, dc * T + qb * 512: dc * T + qb * 512 + 512],
                        start=(dc == 0), stop=(dc == NCH - 1),
                        waits=wl if dc == 0 else (), inc=(dc == NCH - 1))
                evv[qb] = ve(lambda e, pp=ps, q=qb:
                             e.tensor_copy(sb_vf[:, q * 512:(q + 1) * 512], pp[:]),
                             waits=[("s_mm", em.now("s_mm")), ("s_mm", pm)])
            tev = {}
            for tc in range(16):
                pb = PTB[tc % 2]
                wl = [("s_ve", evv[tc // 4]), ("s_sc", psc)]
                if tc >= 2:
                    wl.append(tev[tc - 2])
                trans(pb[:, 0:128], sb_vf[:, tc * 128:(tc + 1) * 128], waits=wl)
                ewl = [("s_mm", em.now("s_mm")), ("s_mm", pm),
                       ("s_ve", st["va_ones"])]
                ve(lambda e, b=pb, t=tc:
                   e.tensor_copy(sb_va[:, t * 130: t * 130 + 64], b[:, 0:64]),
                   waits=ewl)
                tev[tc] = ve(lambda e, b=pb, t=tc:
                             e.tensor_copy(sb_va[:, t * 130 + 65: t * 130 + 129],
                                           b[:, 64:128]))
                tev[tc] = ("s_ve", tev[tc])
            tr_done = em.now("s_mm")
            ncm = {}
            rdone = {}
            rrg = [PS[6][0:1, :], PS[7][0:1, :], SUMAP[0], SUMAP[1]]
            for half in range(2):
                tt_cnt = {}
                for k in range(4):
                    j = half * 4 + k
                    i, qc = divmod(j, NQB)
                    src = sb_q if i == 0 else sb_k
                    sq_sl = sb_vf[:, k * 512: k * 512 + 512]
                    swl = [("s_mm", ncm.get(j - 4, tr_done))]
                    tt_cnt[j] = ve(lambda e, s=src, q=qc, o=sq_sl:
                                   e.tensor_mul(o, s[:, q * 512:(q + 1) * 512],
                                                s[:, q * 512:(q + 1) * 512]),
                                   waits=swl)
                for k in range(4):
                    j = half * 4 + k
                    mwl = [("s_ve", tt_cnt[j]), ("s_dinit", em.now("s_dinit")),
                           ("s_sc", psc)]
                    if j - 4 in rdone:
                        mwl.append(("s_ve", rdone[j - 4]))
                    mme(rrg[k], sb_ones[:], sb_vf[:, k * 512: k * 512 + 512],
                        start=True, stop=True, waits=mwl, inc=True)
                    ncm[j] = em.now("s_mm")
                for k in range(4):
                    j = half * 4 + k
                    rdone[j] = ve(lambda e, pp=rrg[k], col=j:
                                  e.reduce_max(sb_cc[0:1, col:col + 1], pp, axis=AX),
                                  waits=[("s_mm", ncm[j])])
            ve(lambda e: e.reduce_max(sb_cc[0:1, 8:9], sb_cc[0:1, 0:4], axis=AX))
            ve(lambda e: e.memset(sb_cc[0:1, 15:16], 0.0))
            ve(lambda e: e.reduce_max(sb_cc[0:1, 9:10], sb_cc[0:1, 4:8], axis=AX))
            sc(lambda e: e.activation(out=sb_cc[0:1, 10:11], in_=sb_cc[0:1, 8:9],
                                      func=ACT.Sqrt),
               waits=[("s_ve", em.now("s_ve"))])
            sc(lambda e: e.activation(out=sb_cc[0:1, 11:12], in_=sb_cc[0:1, 9:10],
                                      func=ACT.Sqrt))
            ve(lambda e: e.tensor_mul(sb_cc[0:1, 12:13], sb_cc[0:1, 10:11],
                                      sb_cc[0:1, 11:12]),
               waits=[("s_sc", em.now("s_sc"))])
            sc(lambda e: e.mul(sb_cc[0:1, 14:15], sb_cc[0:1, 12:13], -1.0),
               waits=[("s_ve", em.now("s_ve"))])
            par = p % 2
            wl = [("s_sc", em.now("s_sc")), ("s_sc", psc)]
            bcast(sb_cc[0:1, 14:15], sb_negc[:, par: par + 1], "s_db2", waits=wl)
            st["negc"] = em.now("s_db2")
            st["prologue_sc"] = em.now("s_sc")
            st["prologue_ve"] = em.now("s_ve")
            if p + 1 < NCH:
                load_g_fn = st["load_g"]
                load_g_fn(p + 1, st["g_war"].get((p + 1) % 2, 0))

        def qb_scores(p, qb, st, av_prev):
            par = p % 2
            nkc = 4 * qb + 4
            exp_done = {}
            pnkc = av_prev[1] if av_prev else 0
            for kc in range(nkc):
                psW = PSW[kc % 2]
                if kc - 2 in exp_done:
                    wv = exp_done[kc - 2]
                else:
                    tail = st.get("qb_exp_tail")
                    if tail is not None:
                        wv = max(tail[kc], st["prologue_sc"]) if qb == 0 \
                            else tail[kc]
                    else:
                        wv = max(st.get("qb_exp_last", 0), st["prologue_sc"])
                wl = [("s_sc", wv)]
                if qb == 0 and kc < 2:
                    wl.append(("s_ve", st["prologue_ve"]))
                goff = 512 * qb - 128 * kc + 384
                gsl = sb_g[p % 2][:].rearrange("p (h w) -> p h w", w=GW)
                mme(psW[:, 0:512], sb_k[0:64, kc * 128: kc * 128 + 128],
                    sb_q[0:64, qb * 512:(qb + 1) * 512],
                    start=True, stop=True, tp=(0, 0), waits=wl)
                mme(psW[:, 512:1024], sb_k[64:128, kc * 128: kc * 128 + 128],
                    sb_q[64:128, qb * 512:(qb + 1) * 512],
                    start=True, stop=True, tp=(64, 0), inc=True)
                if av_prev and kc < pnkc:
                    qprev, _ = av_prev
                    bank = AVBANK[qprev]
                    for h in range(2):
                        psA = PS[h] if bank == 0 else PS[6 + h]
                        awl = [("s_ve", st["tt_done_prev"][kc])]
                        if kc == 0:
                            awl.append(("s_ve", st["qb_norm_ve"].get(bank, 0)))
                        mme(psA[0:65, :],
                            sb_va[:, kc * 130 + 65 * h: kc * 130 + 65 * h + 65],
                            sb_pt2[:, kc * 1024 + h * 512: kc * 1024 + h * 512 + 512],
                            start=(kc == 0), stop=(kc == pnkc - 1),
                            waits=awl, inc=(kc == pnkc - 1))
                if av_prev and kc == pnkc - 1:
                    qb_norm(p, av_prev[0], st)
                mm_done = em.now("s_mm")
                ewl = [("s_mm", mm_done), ("s_db2", st["negc"]),
                       ("s_ve", st["tt_done"].get(("war", kc), 0))]
                sc(lambda e, pw=psW, k=kc:
                   e.activation(out=sb_pt2[:, k * 1024: k * 1024 + 1024],
                                in_=pw[:], func=ACT.Exp,
                                bias=sb_negc[:, par: par + 1]),
                   waits=ewl)
                exp_done[kc] = em.now("s_sc")
                pt3 = sb_pt2[:, kc * 1024: kc * 1024 + 1024].rearrange(
                    "p (h q) -> p h q", q=512)
                ve(lambda e, p3=pt3, gg=gsl, o=goff:
                   e.tensor_mul(p3, p3, gg[:, :, o: o + 512]),
                   waits=[("s_sc", exp_done[kc]),
                          (f"s_dg{p % 2}", st["g_rdy"][p % 2] if False else 0),
                          ("s_sc", st["g_rdy"][p % 2])])
                st["tt_done"][kc] = em.now("s_ve")
            st["qb_exp_last"] = em.now("s_sc")
            st["qb_exp_tail"] = (exp_done[nkc - 2], exp_done[nkc - 1])
            st["tt_done_prev"] = dict(st["tt_done"])
            return (qb, nkc)

        def qb_av_tail(p, qb, st):
            nkc = 4 * qb + 4
            bank = AVBANK[qb]
            tt_done = st["tt_done_prev"]
            for h in range(2):
                psA = PS[h] if bank == 0 else PS[6 + h]
                for kc in range(nkc):
                    wl = [("s_ve", tt_done[kc])]
                    if kc == 0:
                        wl.append(("s_ve", st["qb_norm_ve"].get(bank, 0)))
                    mme(psA[0:65, :],
                        sb_va[:, kc * 130 + 65 * h: kc * 130 + 65 * h + 65],
                        sb_pt2[:, kc * 1024 + h * 512: kc * 1024 + h * 512 + 512],
                        start=(kc == 0), stop=(kc == nkc - 1),
                        waits=wl, inc=(kc == nkc - 1))
            if p + 1 < NCH:
                st["exp_g"](p + 1)
            qb_norm(p, qb, st)

        def qb_norm(p, qb, st):
            bank = AVBANK[qb]
            PSab = PSA if bank == 0 else PSB
            av_mm = em.now("s_mm")
            buf = qb % 2
            rbuf = sb_rr[0:1, :] if buf == 0 else sb_rr2[0:1, :]
            sc(lambda e, pa=PSab: e.activation(out=sb_rln[0:1, :], in_=pa[64:65, 0:1024],
                                               func=ACT.Ln),
               waits=[("s_mm", av_mm)])
            sc(lambda e, rb=rbuf: e.activation(out=rb, in_=sb_rln[0:1, :],
                                               func=ACT.Exp, scale=-1.0),
               waits=[("s_dbw", st.get("rrw", 0))])
            bs = "s_db0" if buf == 0 else "s_db1"
            bcast(rbuf, sb_r64c[buf][:, :], bs, waits=[("s_sc", em.now("s_sc"))])
            st["rrw"] = em.now("s_dbw")
            bn = em.now(bs)
            for h in range(2):
                dsl = sb_ao[64 * h: 64 * h + 64, p * T + qb * 512: p * T + qb * 512 + 512]
                ve(lambda e, pa=PSab, d=dsl, hh=h, bf=buf:
                   e.tensor_mul(d, pa[0:64, hh * 512:(hh + 1) * 512],
                                sb_r64c[bf][:, hh * 512:(hh + 1) * 512]),
                   waits=[(bs, bn)])
            st["qb_norm_ve"][bank] = em.now("s_ve")

        sb_rr2 = ph2.enter_context(nc.sbuf_tensor("sb_rr2", [1, 2 * QB], F32))
        sb_rln = ph2.enter_context(nc.sbuf_tensor("sb_rln", [1, 2 * QB], F32))
        sb_e1 = ph2.enter_context(nc.sbuf_tensor("sb_e1", [1, 64], F32))
        sb_r64c = [ph2.enter_context(nc.sbuf_tensor(f"sb_r64c{i}", [64, 2 * QB], F32))
                   for i in range(2)]

        if PHASE_LIMIT >= 2:
            em.run_phase("attn", attention)
        ph2.close()

        ph3 = ExitStack()
        sb_R0b = ph3.enter_context(nc.sbuf_tensor("sb_R0b", [1, T], F32))
        sb_R1b = ph3.enter_context(nc.sbuf_tensor("sb_R1b", [1, T], F32))
        sb_R2b = ph3.enter_context(nc.sbuf_tensor("sb_R2b", [1, T], F32))
        sb_mrep3 = ph3.enter_context(nc.sbuf_tensor("sb_mrep3", [128, T], F32))
        sb_rrep3 = ph3.enter_context(nc.sbuf_tensor("sb_rrep3", [128, T], F32))
        sb_x3 = [ph3.enter_context(nc.sbuf_tensor(f"sb_x3{i}", [128, T], F32)) for i in range(2)]
        sb_x23 = [ph3.enter_context(nc.sbuf_tensor(f"sb_x23{i}", [128, T], F32)) for i in range(2)]
        sb_sq3 = ph3.enter_context(nc.sbuf_tensor("sb_sq3", [128, T], BF16))
        sb_xb3 = ph3.enter_context(nc.sbuf_tensor("sb_xb3", [128, T], BF16))

        def phase3():
            em.guard_all()
            attn_ve = em.now("s_ve")
            dma("gpsimd", sb3(WB[0], 1024), wchunk(wo), "s_dw0",
                waits=[("s_mm", em.now("s_mm"))])
            ev = {}
            for oc in range(NCH):
                for tb in range(NQB):
                    ps = PS[0] if (oc * NQB + tb) % 2 == 0 else PS[1]
                    wl = [("s_dw0", em.now("s_dw0")), ("s_ve", attn_ve)]
                    if (oc * NQB + tb) >= 2:
                        wl.append(("s_sc", ev[(oc * NQB + tb) - 2]))
                    for fc in range(NCH):
                        mme(ps[:], WB[0][:, fc * 1024 + oc * 128: fc * 1024 + oc * 128 + 128],
                            sb_ao[:, fc * T + tb * 512: fc * T + tb * 512 + 512],
                            start=(fc == 0), stop=(fc == NCH - 1),
                            waits=wl if fc == 0 else (), inc=(fc == NCH - 1))
                    ev[oc * NQB + tb] = sc(
                        lambda e, pp=ps, o=oc, t=tb:
                        e.add(sb_lnx[:, o * T + t * 512: o * T + t * 512 + 512],
                              pp[:], sb_bo[:, o:o + 1]),
                        waits=[("s_mm", em.now("s_mm"))])

            stat_mm = {}
            for c in range(NCH):
                swl = [("s_sc", ev[c * NQB + NQB - 1]), ("s_mm", stat_mm.get(c - 1, 0))]
                sc(lambda e, cc=c: e.activation(out=sb_sq3[:],
                                                in_=sb_lnx[:, cc * T:(cc + 1) * T],
                                                func=ACT.Square), waits=swl)
                scn = em.now("s_sc")
                for qc in range(NQB):
                    wl2 = [("s_sc", scn)]
                    if c == 0:
                        wl2.append(("s_ve", em.now("s_ve")))
                    mme(SUMAP[qc], sb_ones[:],
                        sb_lnx[:, c * T + qc * 512: c * T + qc * 512 + 512],
                        start=(c == 0), stop=(c == NCH - 1),
                        waits=wl2, inc=(c == NCH - 1))
                for qc in range(NQB):
                    mme(SQAP[qc], sb_ones[:], sb_sq3[:, qc * 512:(qc + 1) * 512],
                        start=(c == 0), stop=(c == NCH - 1),
                        inc=(c == NCH - 1 or qc == NQB - 1))
                stat_mm[c] = em.now("s_mm")
            ln_rows(sb_R0b, sb_R1b, sb_R2b, T,
                    [(SUMAP[qc], qc * 512, 512) for qc in range(NQB)],
                    [(SQAP[qc], qc * 512, 512) for qc in range(NQB)],
                    sb_mrep3, sb_rrep3, "s_db0", "s_db1")
            if DEBUG_DUMPS:
                dma("sync", dumps["d_m3"], sb_R0b[0:1, :], "s_dst",
                    waits=[("s_ve", em.now("s_ve")), ("s_sc", em.now("s_sc"))])
                dma("sync", dumps["d_r3"], sb_R2b[0:1, :], "s_dst")

            x2w = {}
            stat2_mm = {}
            tt_done = {}
            for c in range(NCH):
                sl = c % 2
                sxx = "s_dxc0" if sl == 0 else "s_dxc1"
                x2sl = sb_x23[sl][:]
                dma("sync", sb_x3[sl][:], xTc[c], sxx,
                    waits=[("s_ve", tt_done.get(c - 2, 0))])
                wl = [(sxx, em.now(sxx)), ("s_db0", em.now("s_db0")),
                      ("s_db1", em.now("s_db1")), ("s_dx2w", x2w.get(c - 2, 0)),
                      ("s_mm", em.now("s_mm"))]
                ve(lambda e, cc=c, x2b=x2sl: e.tensor_sub(
                    x2b, sb_lnx[:, cc * T:(cc + 1) * T], sb_mrep3[:]), waits=wl)
                ve(lambda e, x2b=x2sl: e.tensor_mul(x2b, x2b, sb_rrep3[:]))
                ve(lambda e, s=sl, x2b=x2sl: e.tensor_add(x2b, x2b, sb_x3[s][:]))
                tt_done[c] = em.now("s_ve")
                x2w[c] = dma("gpsimd", x2s[c], x2sl, "s_dx2w",
                             waits=[("s_ve", em.now("s_ve"))])
                sc(lambda e, x2b=x2sl: e.copy(sb_xb3[:], x2b),
                   waits=[("s_ve", em.now("s_ve")), ("s_mm", stat2_mm.get(c - 1, 0))])
                sc(lambda e: e.activation(out=sb_sq3[:], in_=sb_xb3[:], func=ACT.Square))
                scn = em.now("s_sc")
                for qc in range(NQB):
                    wl2 = [("s_sc", scn)]
                    if c == 0:
                        wl2.append(("s_ve", em.now("s_ve")))
                    mme(SUMAP[qc], sb_ones[:], sb_xb3[:, qc * 512:(qc + 1) * 512],
                        start=(c == 0), stop=(c == NCH - 1),
                        waits=wl2, inc=(c == NCH - 1))
                for qc in range(NQB):
                    mme(SQAP[qc], sb_ones[:], sb_sq3[:, qc * 512:(qc + 1) * 512],
                        start=(c == 0), stop=(c == NCH - 1),
                        inc=(c == NCH - 1 or qc == NQB - 1))
                stat2_mm[c] = em.now("s_mm")
            ln_rows(sb_R0b, sb_R1b, sb_R2b, T,
                    [(SUMAP[qc], qc * 512, 512) for qc in range(NQB)],
                    [(SQAP[qc], qc * 512, 512) for qc in range(NQB)],
                    sb_mrep3, sb_rrep3, "s_db2", "s_db3")
            if DEBUG_DUMPS:
                dma("sync", dumps["d_m3b"], sb_R0b[0:1, :], "s_dst",
                    waits=[("s_ve", em.now("s_ve")), ("s_sc", em.now("s_sc"))])
                dma("sync", dumps["d_r3b"], sb_R2b[0:1, :], "s_dst")
            app_tt = {}
            for c in range(NCH):
                sl = c % 2
                sxx = "s_dxc0" if sl == 0 else "s_dxc1"
                dma("sync", sb_x3[sl][:], x2s[c], sxx,
                    waits=[("s_ve", app_tt.get(c - 2, 0)), ("s_dx2w", x2w[c])])
                wl = [(sxx, em.now(sxx)), ("s_db2", em.now("s_db2")),
                      ("s_db3", em.now("s_db3"))]
                ve(lambda e, s=sl: e.tensor_sub(sb_x23[0][:], sb_x3[s][:], sb_mrep3[:]),
                   waits=wl)
                app_tt[c] = ve(lambda e, cc=c: e.tensor_mul(
                    sb_lnx[:, cc * T:(cc + 1) * T], sb_x23[0][:], sb_rrep3[:]))
            if DEBUG_DUMPS:
                dma("gpsimd", dumps["d_ln2"], sb_lnx[:], "s_dst",
                    waits=[("s_ve", em.now("s_ve"))])

        if PHASE_LIMIT >= 3:
            em.run_phase("wo_ln2", phase3)
        ph3.close()
        phA.close()

        ph4 = ExitStack()
        sb_lgw = ph4.enter_context(nc.sbuf_tensor("sb_lgw", [128, 16 * 15], F32))
        sb_exw = ph4.enter_context(nc.sbuf_tensor("sb_exw", [128, 16 * 15], F32))
        sb_cww = ph4.enter_context(nc.sbuf_tensor("sb_cww", [128, 16 * 15], F32))
        sb_t8w = ph4.enter_context(nc.sbuf_tensor("sb_t8w", [128, 16 * 8], F32))
        sb_ssm = ph4.enter_context(nc.sbuf_tensor("sb_ssm", [128, 16], F32))
        sb_sss = ph4.enter_context(nc.sbuf_tensor("sb_sss", [128, 16], F32))
        sb_ssr = ph4.enter_context(nc.sbuf_tensor("sb_ssr", [128, 16], F32))
        sb_cwp = ph4.enter_context(nc.sbuf_tensor("sb_cwp", [128, 128], BF16))

        def phase4():
            em.guard_all()
            ve(lambda e: e.memset(sb_cwp[:], 0.0))
            add_ve = {}
            for tt in range(16):
                ps = PS[6] if tt % 2 == 0 else PS[7]
                wl = [("s_dcw0", em.now("s_dcw0")), ("s_ve", add_ve.get(tt - 2, 0)),
                      ("s_ve", em.now("s_ve") if tt == 0 else 0)]
                for dc in range(NCH):
                    mme(ps[0:128, 0:15],
                        sb_lnx[:, dc * T + tt * 128: dc * T + tt * 128 + 128],
                        sb_rw[:, dc * E: dc * E + E],
                        start=(dc == 0), stop=(dc == NCH - 1),
                        waits=wl if dc == 0 else (), inc=(dc == NCH - 1))
                add_ve[tt] = ve(
                    lambda e, pp=ps, t=tt: e.tensor_add(
                        sb_lgw[:, t * 15: t * 15 + 15], pp[0:128, 0:15], sb_rb[:, :]),
                    waits=[("s_mm", em.now("s_mm")), ("s_dinit", em.now("s_dinit"))])
            for tt in range(16):
                ve(lambda e, t=tt: e.max(out=sb_t8w[:, t * 8: t * 8 + 8],
                                         in_=sb_lgw[:, t * 15: t * 15 + 15]))
            lg3 = sb_lgw[:].rearrange("p (t e) -> p t e", e=15)
            ve(lambda e: e.tensor_reduce(sb_ssm[:].rearrange("p (t o) -> p t o", o=1),
                                         lg3, AX, ALU.max, negate=True))
            for tt in range(16):
                sc(lambda e, t=tt: e.activation(
                    out=sb_exw[:, t * 15: t * 15 + 15],
                    in_=sb_lgw[:, t * 15: t * 15 + 15], func=ACT.Exp,
                    bias=sb_ssm[:, t: t + 1], accum_out=sb_sss[:, t: t + 1]),
                   waits=[("s_ve", em.now("s_ve"))] if tt == 0 else ())
            ve(lambda e: e.reciprocal(sb_ssr[:], sb_sss[:]),
               waits=[("s_sc", em.now("s_sc"))])
            for tt in range(16):
                ve(lambda e, t=tt: e.tensor_single_scalar(
                    sb_lgw[:, t * 15: t * 15 + 15], sb_lgw[:, t * 15: t * 15 + 15],
                    sb_t8w[:, t * 8 + 2: t * 8 + 3], ALU.is_ge))
            ve(lambda e: e.tensor_mul(sb_exw[:], sb_exw[:], sb_lgw[:]))
            for tt in range(16):
                ve(lambda e, t=tt: e.tensor_scalar_mul(
                    sb_cww[:, t * 15: t * 15 + 15], sb_exw[:, t * 15: t * 15 + 15],
                    sb_ssr[:, t: t + 1]))
            cw_ev = {}
            trans_mm = {}
            for tt in range(16):
                ve(lambda e, t=tt: e.tensor_copy(sb_cwp[:, 0:15],
                                                 sb_cww[:, t * 15: t * 15 + 15]),
                   waits=[("s_mm", trans_mm.get(tt - 1, 0))])
                pb = PTB[0] if tt % 2 == 0 else PTB[1]
                twl = [("s_ve", em.now("s_ve")), ("s_dinit", em.now("s_dinit"))]
                if tt >= 2:
                    twl.append(("s_sc", cw_ev[tt - 2]))
                trans(pb[:, 0:128], sb_cwp[:], waits=twl)
                trans_mm[tt] = em.now("s_mm")
                cw_ev[tt] = sc(lambda e, b=pb, t=tt:
                               e.copy(sb_cw[0:15, t * 128:(t + 1) * 128], b[0:15, 0:128]),
                               waits=[("s_mm", em.now("s_mm"))])
            if DEBUG_DUMPS:
                wl = [("s_sc", em.now("s_sc")), ("s_ve", em.now("s_ve"))]
                dma("gpsimd", dumps["d_cw"], sb_cw[:], "s_dst", waits=wl)
                dma("gpsimd", dumps["d_cwp"], sb_cwp[:], "s_dst", waits=wl)
                dma("sync", dumps["d_lg"], sb_lgw[:, 15 * 15:16 * 15], "s_dst", waits=wl)
                dma("sync", dumps["d_t8"], sb_t8w[:, 15 * 8:16 * 8], "s_dst", waits=wl)

        if PHASE_LIMIT >= 4:
            em.run_phase("router", phase4)
        ph4.close()

        ph5 = ExitStack()
        sb_H = [ph5.enter_context(nc.sbuf_tensor(f"sb_H{i}", [128, NCH * TH], BF16))
                for i in range(2)]
        sb_rt = ph5.enter_context(nc.sbuf_tensor("sb_rt", [128, NCH * TH], F32))
        sb_cwr = ph5.enter_context(nc.sbuf_tensor("sb_cwr", [128, TH], BF16))
        sb_R05 = ph5.enter_context(nc.sbuf_tensor("sb_R05", [1, TH], F32))
        sb_R15 = ph5.enter_context(nc.sbuf_tensor("sb_R15", [1, TH], F32))
        sb_R25 = ph5.enter_context(nc.sbuf_tensor("sb_R25", [1, TH], F32))
        sb_mrep5 = ph5.enter_context(nc.sbuf_tensor("sb_mrep5", [128, TH], F32))
        sb_rrep5 = ph5.enter_context(nc.sbuf_tensor("sb_rrep5", [128, TH], F32))
        sb_t5 = ph5.enter_context(nc.sbuf_tensor("sb_t5", [128, TH], F32))
        sb_sq5 = ph5.enter_context(nc.sbuf_tensor("sb_sq5", [128, TH], BF16))
        sb_xb5 = ph5.enter_context(nc.sbuf_tensor("sb_xb5", [128, TH], BF16))
        sb_ob = [ph5.enter_context(nc.sbuf_tensor(f"sb_ob{i}", [128, TH], F32)) for i in range(2)]
        sb_x25 = [ph5.enter_context(nc.sbuf_tensor(f"sb_x25{i}", [128, TH], F32)) for i in range(2)]

        wslot_war = {}

        def ffn_half(half, wst):
            tok0 = half * TH

            def load_w(e_idx, which, slot):
                sem = f"s_dw{slot}"
                src = (sw1 if which == 0 else sw2) if e_idx < 0 else \
                    (ew1 if which == 0 else ew2)[e_idx]
                war = wst.get(slot, 0)
                dma("gpsimd", sb3(WB[slot], 1024), wchunk(src),
                    sem, waits=[("s_mm", war)] if war else ())
                return em.now(sem)

            cyc = [(0, 1), (2, 0), (1, 2)]
            rdy = {}
            slot_of = {}
            for e_i in range(-1, E):
                slot_of[e_i] = cyc[(e_i + 1) % 3]
            rdy[-1] = (load_w(-1, 0, slot_of[-1][0]), load_w(-1, 1, slot_of[-1][1]))

            for e_i in range(-1, E):
                s1, s2 = slot_of[e_i]
                r1, r2 = rdy[e_i]
                hbuf = sb_H[(e_i + 1) % 2]
                hbuf_war = wst.get(f"h{(e_i + 1) % 2}", 0)
                if e_i + 1 < E:
                    n1, n2 = slot_of[e_i + 1]
                    rdy1_next = load_w(e_i + 1, 0, n1)
                bs = None
                cw_rdy = 0
                if e_i >= 0:
                    bs = "s_db0" if e_i % 2 == 0 else "s_db1"
                    cw_rdy = bcast(sb_cw[e_i:e_i + 1, tok0: tok0 + TH], sb_cwr[:, :], bs,
                                   waits=[("s_ve", wst.get("cw_war", 0)),
                                          ("s_sc", wst["cw_sc"])])
                h_ev = {}
                h_rdy = {}
                for tc in range(2):
                    for hc in range(NCH):
                        k = tc * NCH + hc
                        ps = PS[0] if k % 2 == 0 else PS[1]
                        wl = [(f"s_dw{s1}", r1), ("s_mm", hbuf_war)]
                        if k >= 2:
                            wl.append(("s_sc", h_ev[k - 2]))
                        for dc in range(NCH):
                            mme(ps[:], WB[s1][:, dc * 1024 + hc * 128: dc * 1024 + hc * 128 + 128],
                                sb_lnx[:, dc * T + tok0 + tc * 512: dc * T + tok0 + tc * 512 + 512],
                                start=(dc == 0), stop=(dc == NCH - 1),
                                waits=wl if dc == 0 else (), inc=(dc == NCH - 1))
                        bias = sb_sb1[:, hc:hc + 1] if e_i < 0 else \
                            sb_eb1[:, e_i * NCH + hc: e_i * NCH + hc + 1]
                        h_ev[k] = sc(
                            lambda e, pp=ps, h=hc, t=tc, b=bias, hb=hbuf:
                            e.activation(out=hb[:, h * TH + t * 512: h * TH + t * 512 + 512],
                                         in_=pp[:], func=ACT.Relu, bias=b),
                            waits=[("s_mm", em.now("s_mm"))])
                    if e_i >= 0:
                        hb3 = hbuf[:].rearrange("p (h t) -> p h t", t=TH)
                        cw3 = sb_cwr[:].rearrange("p (o t) -> p o t", o=1)
                        ve(lambda e, h3=hb3, c3=cw3, t=tc:
                           e.tensor_mul(h3[:, :, t * 512:(t + 1) * 512],
                                        h3[:, :, t * 512:(t + 1) * 512],
                                        c3[:, :, t * 512:(t + 1) * 512]
                                        .to_broadcast([128, NCH, 512])),
                           waits=[("s_sc", h_ev[tc * NCH + NCH - 1]), (bs, cw_rdy)])
                        h_rdy[tc] = ("s_ve", em.now("s_ve"))
                    else:
                        h_rdy[tc] = ("s_sc", h_ev[tc * NCH + NCH - 1])
                if e_i >= 0:
                    wst["cw_war"] = em.now("s_ve")
                wst[s1] = em.now("s_mm")
                if e_i + 1 < E:
                    rdy[e_i + 1] = (rdy1_next, load_w(e_i + 1, 1, n2))
                w2_ev = {}
                for tc in range(2):
                    for oc in range(NCH):
                        k = tc * NCH + oc
                        psap = W2P[k % 2]
                        wl = [(f"s_dw{s2}", r2), h_rdy[tc]]
                        if k >= 2:
                            wl.append(w2_ev[k - 2])
                        for hc in range(NCH):
                            mme(psap, WB[s2][:, hc * 1024 + oc * 128: hc * 1024 + oc * 128 + 128],
                                hbuf[:, hc * TH + tc * 512: hc * TH + tc * 512 + 512],
                                start=(hc == 0), stop=(hc == NCH - 1 and e_i >= 0),
                                waits=wl if hc == 0 else (), inc=(hc == NCH - 1 and e_i >= 0))
                        rt_sl = sb_rt[:, oc * TH + tc * 512: oc * TH + tc * 512 + 512]
                        if e_i < 0:
                            mme(psap, sb_eb2[0:15, oc * 128: oc * 128 + 128],
                                sb_cw[0:15, tok0 + tc * 512: tok0 + tc * 512 + 512],
                                start=False, stop=True,
                                waits=[("s_dcw1", em.now("s_dcw1"))], inc=True)
                            w2_ev[k] = ("s_sc", sc(
                                lambda e, pp=psap, o=oc, r=rt_sl:
                                e.add(r, pp, sb_sb2[:, o:o + 1]),
                                waits=[("s_mm", em.now("s_mm")),
                                       ("s_ve", wst.get("rt_read", 0))]))
                        else:
                            w2_ev[k] = ("s_ve", ve(
                                lambda e, pp=psap, r=rt_sl:
                                e.tensor_add(r, pp, r),
                                waits=[("s_mm", em.now("s_mm")),
                                       ("s_sc", em.now("s_sc")),
                                       ("s_ve", wst.get("rt_war", 0))]))
                wst[s2] = em.now("s_mm")
                wst[f"h{(e_i + 1) % 2}"] = em.now("s_mm")

            acc_ve = em.now("s_ve")
            stat5_mm = {}
            for c in range(NCH):
                wl = [("s_ve", acc_ve), ("s_mm", em.now("s_mm"))] if c == 0 else \
                    [("s_mm", stat5_mm.get(c - 1, 0))]
                sc(lambda e, cc=c: e.copy(sb_xb5[:], sb_rt[:, cc * TH:(cc + 1) * TH]),
                   waits=wl)
                sc(lambda e: e.activation(out=sb_sq5[:], in_=sb_xb5[:], func=ACT.Square))
                scn = em.now("s_sc")
                for qc in range(2):
                    ps = PS[6] if qc == 0 else PS[7]
                    mme(ps[0:1, :], sb_ones[:], sb_xb5[:, qc * 512:(qc + 1) * 512],
                        start=(c == 0), stop=(c == NCH - 1),
                        waits=[("s_sc", scn), ("s_ve", wst.get("ln_war", 0))],
                        inc=(c == NCH - 1))
                    pq = SUMAP[2] if qc == 0 else SUMAP[3]
                    mme(pq, sb_ones[:], sb_sq5[:, qc * 512:(qc + 1) * 512],
                        start=(c == 0), stop=(c == NCH - 1),
                        inc=(c == NCH - 1 or qc == 1))
                stat5_mm[c] = em.now("s_mm")
            ln_rows(sb_R05, sb_R15, sb_R25, TH,
                    [(PS[6][0:1, :], 0, 512), (PS[7][0:1, :], 512, 512)],
                    [(SUMAP[2], 0, 512), (SUMAP[3], 512, 512)],
                    sb_mrep5, sb_rrep5, "s_db2", "s_db3")
            wst["ln_war"] = em.now("s_ve")
            st_done = {}
            out_tt = {}
            for c in range(NCH):
                sl = c % 2
                sx = "s_dxr0" if sl == 0 else "s_dxr1"
                rwl = [("s_ve", out_tt.get(c - 2, 0))] if c >= 2 else \
                    [("s_dx2w", em.now("s_dx2w")), ("s_ve", em.now("s_ve")),
                     ("s_dst", em.now("s_dst"))]
                dma("sync", sb_x25[sl][:], x2s[c, :, tok0: tok0 + TH], sx, waits=rwl)
                wl = [(sx, em.now(sx)), ("s_db2", em.now("s_db2")),
                      ("s_db3", em.now("s_db3"))]
                if c >= 2:
                    wl.append(("s_dst", st_done[c - 2]))
                ve(lambda e, cc=c: e.tensor_sub(sb_t5[:], sb_rt[:, cc * TH:(cc + 1) * TH],
                                                sb_mrep5[:]), waits=wl)
                ve(lambda e: e.tensor_mul(sb_t5[:], sb_t5[:], sb_rrep5[:]))
                out_tt[c] = ve(lambda e, s=sl: e.tensor_add(sb_ob[s][:], sb_t5[:],
                                                            sb_x25[s][:]))
                st_done[c] = dma("sync", out_h[c, :, tok0: tok0 + TH], sb_ob[sl][:],
                                 "s_dst", waits=[("s_ve", em.now("s_ve"))])
            wst["rt_war"] = em.now("s_ve")
            wst["rt_read"] = em.now("s_ve")

        def phase5():
            em.guard_all()
            wst = {i: em.now("s_mm") for i in range(3)}
            wst["h0"] = em.now("s_mm")
            wst["h1"] = em.now("s_mm")
            wst["rt_read"] = 0
            wst["rt_war"] = em.now("s_ve")
            wst["cw_sc"] = em.now("s_sc")
            ffn_half(0, wst)
            ffn_half(1, wst)
            if em.act("sync", [("s_dst", em.now("s_dst"))]):
                pass

        if PHASE_LIMIT >= 5:
            em.run_phase("ffn", phase5)
        ph5.close()

        def drain():
            em.guard_all()

        em.run_phase("drain", drain)

    return nc


def prep_inputs(inputs):
    f32 = np.float32
    x = np.asarray(inputs["L0_attn_x"], dtype=f32)
    tab = np.asarray(inputs["L0_attn_rel_pos_table"], dtype=f32)

    def cb(b):
        return np.ascontiguousarray(np.asarray(b, dtype=f32).reshape(NCH, 128).T)

    shared = {
        "wq": np.ascontiguousarray(np.asarray(inputs["L0_attn_WQ"], dtype=f32).T),
        "wk": np.ascontiguousarray(np.asarray(inputs["L0_attn_WK"], dtype=f32).T),
        "wv": np.ascontiguousarray(np.asarray(inputs["L0_attn_WV"], dtype=f32).T),
        "wo": np.ascontiguousarray(np.asarray(inputs["L0_attn_WO"], dtype=f32).T),
        "sw1": np.ascontiguousarray(np.asarray(inputs["L0_ffn_sW1"], dtype=f32)),
        "sw2": np.ascontiguousarray(np.asarray(inputs["L0_ffn_sW2"], dtype=f32)),
        "ew1": np.ascontiguousarray(np.asarray(inputs["L0_ffn_eW1"], dtype=f32)),
        "ew2": np.ascontiguousarray(np.asarray(inputs["L0_ffn_eW2"], dtype=f32)),
        "rw": np.ascontiguousarray(np.asarray(inputs["L0_ffn_router_W"], dtype=f32)),
        "bqc": cb(inputs["L0_attn_bQ"]), "bkc": cb(inputs["L0_attn_bK"]),
        "bvc": cb(inputs["L0_attn_bV"]), "boc": cb(inputs["L0_attn_bO"]),
        "sb1c": cb(inputs["L0_ffn_sb1"]), "sb2c": cb(inputs["L0_ffn_sb2"]),
        "eb1r": np.ascontiguousarray(
            np.asarray(inputs["L0_ffn_eb1"], dtype=f32).reshape(E, NCH, 128)
            .transpose(2, 0, 1).reshape(128, E * NCH)),
        "eb2n": np.ascontiguousarray(np.asarray(inputs["L0_ffn_eb2"], dtype=f32)),
        "rbrep": np.ascontiguousarray(np.broadcast_to(
            np.asarray(inputs["L0_ffn_router_b"], dtype=f32)[None, :], (128, E))),
        "onesb": np.ones((128, 1), dtype=ml_dtypes.bfloat16),
        "indb": np.ascontiguousarray(np.concatenate(
            [np.tile(np.array([[1, 0]], f32), (64, 1)),
             np.tile(np.array([[0, 1]], f32), (64, 1))], axis=0)
            .astype(ml_dtypes.bfloat16)),
        "identb": np.eye(128, dtype=ml_dtypes.bfloat16),
        "id64db": np.ascontiguousarray(np.concatenate(
            [np.eye(64), np.eye(64)], axis=0).astype(ml_dtypes.bfloat16)),
    }
    pidx = np.arange(128)[:, None]
    midx = np.arange(GW)[None, :]
    idx = 1663 + midx - pidx
    g = tab[idx, :]
    g = np.ascontiguousarray(g.transpose(2, 0, 1))
    g[:, idx < 2047] = -1e30
    shared["g"] = g

    per_core = []
    for c in range(8):
        d = dict(shared)
        d["xT"] = np.ascontiguousarray(x[c].T)
        per_core.append(d)
    return per_core


def kernel(**inputs):
    in_maps = prep_inputs(inputs)
    nc = build_program()
    res = run_bass_kernel_spmd(nc, in_maps, core_ids=list(range(8)))
    outs = []
    for c in range(8):
        o = res.results[c]["out"].reshape(D, T)
        outs.append(np.ascontiguousarray(o.T))
    return np.stack(outs, axis=0).astype(np.float32)


# version 20 (final)
# speedup vs baseline: 1.0080x; 1.0080x over previous
import numpy as np
import ml_dtypes
from contextlib import ExitStack

import concourse.bass as bass
import concourse.mybir as mybir
from concourse.bass_utils import run_bass_kernel_spmd

F32 = mybir.dt.float32
BF16 = mybir.dt.bfloat16
AX = mybir.AxisListType.X
ALU = mybir.AluOpType
ACT = mybir.ActivationFunctionType

D = 1024
T = 2048
NCH = 8
NH = 16
E = 15
QB = 512
NQB = T // QB
TH = 1024
GW = 2432
EPS = 1e-5

DEBUG_DUMPS = False
PHASE_LIMIT = 5

ENGS = ["sync", "gpsimd", "tensor", "vector", "scalar"]
COMPUTE_OWNER = {"s_mm": "tensor", "s_ve": "vector", "s_sc": "scalar"}


class Em:
    def __init__(self, nc, ctx, semnames):
        self.nc = nc
        self.sems = {n: ctx.enter_context(nc.semaphore(n)) for n in semnames}
        self.count = {n: 0 for n in semnames}
        self.hw = {e: {} for e in ENGS}
        self.cur = None
        self.eng = None

    def now(self, sem):
        return self.count[sem]

    def act(self, eng, waits=()):
        on = self.cur == eng
        for (s, v) in waits:
            if v <= 0 or COMPUTE_OWNER.get(s) == eng:
                continue
            if on and v > self.hw[eng].get(s, 0):
                self.eng.wait_ge(self.sems[s], v)
                self.hw[eng][s] = v
        return on

    def fin(self, ins, sem, amt):
        self.count[sem] += amt
        if ins is not None:
            ins.then_inc(self.sems[sem], amt)
        return self.count[sem]

    def guard_all(self):
        for n in self.sems:
            v = self.count[n]
            if v <= 0 or COMPUTE_OWNER.get(n) == self.cur:
                continue
            if v > self.hw[self.cur].get(n, 0):
                self.eng.wait_ge(self.sems[n], v)
                self.hw[self.cur][n] = v

    def run_phase(self, name, fn):
        snap = dict(self.count)
        hw_snap = {e: dict(self.hw[e]) for e in ENGS}
        final = [None]
        with self.nc.Block(name) as blk:
            for engname in ENGS:
                self.count = dict(snap)
                self.hw[engname] = dict(hw_snap[engname])
                self.cur = engname

                def body(eng, _fn=fn):
                    self.eng = eng
                    _fn()
                    final[0] = dict(self.count)

                getattr(blk, engname)(body)
        self.count = final[0]
        self.cur = None


def build_program():
    nc = bass.Bass("TRN2", target_bir_lowering=False, debug=False)

    def din(name, shape, dtype=F32):
        return nc.dram_tensor(name, shape, dtype, kind="ExternalInput").ap()

    xT = din("xT", [D, T])
    wq = din("wq", [D, D]); wk = din("wk", [D, D])
    wv = din("wv", [D, D]); wo = din("wo", [D, D])
    sw1 = din("sw1", [D, D]); sw2 = din("sw2", [D, D])
    ew1 = din("ew1", [E, D, D]); ew2 = din("ew2", [E, D, D])
    rw = din("rw", [D, E])
    gH = din("g", [NH, 128, GW])
    bqc = din("bqc", [128, NCH]); bkc = din("bkc", [128, NCH])
    bvc = din("bvc", [128, NCH]); boc = din("boc", [128, NCH])
    sb1c = din("sb1c", [128, NCH]); sb2c = din("sb2c", [128, NCH])
    eb1r = din("eb1r", [128, E * NCH]); eb2n = din("eb2n", [E, D])
    rbrep = din("rbrep", [128, E])
    onesb = din("onesb", [128, 1], BF16)
    indb = din("indb", [128, 2], BF16)
    identb = din("identb", [128, 128], BF16)
    id64db = din("id64db", [128, 64], BF16)

    out_h = nc.dram_tensor("out", [NCH, 128, T], F32, kind="ExternalOutput").ap()
    _dk = "ExternalOutput" if DEBUG_DUMPS else "Internal"
    x2s = nc.dram_tensor("x2s", [NCH, 128, T], F32, kind=_dk).ap()
    bscr = nc.dram_tensor("bscr", [4, T], F32, kind="Internal").ap()
    bscrb = nc.dram_tensor("bscrb", [4, T], BF16, kind="Internal").ap()
    oOs = nc.dram_tensor("oOs", [NCH, 128, T], BF16, kind=_dk).ap()

    dumps = {}
    if DEBUG_DUMPS:
        for name, shape in [
            ("d_ln1", [128, NCH * T]), ("d_q", [128, T]), ("d_k", [128, T]),
            ("d_va", [128, 16 * 130]), ("d_pt0", [128, 16 * QB]),
            ("d_attno", [128, NCH * T]), ("d_ln2", [128, NCH * T]),
            ("d_cw", [16, T]), ("d_negc", [128, 2]),
            ("d_m3", [1, T]), ("d_r3", [1, T]), ("d_m3b", [1, T]), ("d_r3b", [1, T]),
            ("d_R0", [1, T]), ("d_R1", [1, T]), ("d_R2", [1, T]),
            ("d_cwp", [128, 128]), ("d_lg", [128, 15]), ("d_t8", [128, 8]),
            ("d_mr1", [128, T]), ("d_rr1", [128, T]),
        ]:
            dumps[name] = nc.dram_tensor(name, shape, F32, kind="ExternalOutput").ap()

    semnames = (
        ["s_mm", "s_ve", "s_sc"]
        + ["s_dinit", "s_dw0", "s_dw1", "s_dw2"]
        + ["s_dg0", "s_dg1", "s_dg2", "s_dtb0", "s_dtb1"]
        + ["s_db0", "s_db1", "s_db2", "s_db3", "s_dbw"]
        + ["s_dxr0", "s_dxr1", "s_dor0", "s_dor1", "s_dx2w", "s_doow",
           "s_dst", "s_dcw0", "s_dcw1", "s_dxc0", "s_dxc1"]
    )

    with ExitStack() as ctx:
        em = Em(nc, ctx, semnames)
        alloc = ctx.enter_context

        sb_bq = alloc(nc.sbuf_tensor("sb_bq", [128, NCH], F32))
        sb_bk = alloc(nc.sbuf_tensor("sb_bk", [128, NCH], F32))
        sb_bv = alloc(nc.sbuf_tensor("sb_bv", [128, NCH], F32))
        sb_bo = alloc(nc.sbuf_tensor("sb_bo", [128, NCH], F32))
        sb_sb1 = alloc(nc.sbuf_tensor("sb_sb1", [128, NCH], F32))
        sb_sb2 = alloc(nc.sbuf_tensor("sb_sb2", [128, NCH], F32))
        sb_eb1 = alloc(nc.sbuf_tensor("sb_eb1", [128, E * NCH], F32))
        sb_rb = alloc(nc.sbuf_tensor("sb_rb", [128, E], F32))
        sb_eb2 = alloc(nc.sbuf_tensor("sb_eb2", [E, D], BF16))
        sb_rw = alloc(nc.sbuf_tensor("sb_rw", [128, NCH * E], BF16))
        sb_ones = alloc(nc.sbuf_tensor("sb_ones", [128, 1], BF16))
        sb_ind = alloc(nc.sbuf_tensor("sb_ind", [128, 2], BF16))
        sb_id = alloc(nc.sbuf_tensor("sb_id", [128, 128], BF16))
        sb_id64 = alloc(nc.sbuf_tensor("sb_id64", [128, 64], BF16))
        sb_bq8 = alloc(nc.sbuf_tensor("sb_bq8", [128, NCH], F32))
        sb_negc = alloc(nc.sbuf_tensor("sb_negc", [128, 2], F32))
        sb_cw = alloc(nc.sbuf_tensor("sb_cw", [16, T], BF16))
        sb_qm = alloc(nc.sbuf_tensor("sb_qm", [2, 16], F32))
        sb_eps = alloc(nc.sbuf_tensor("sb_eps", [1, 1], F32))

        PSA = alloc(nc.psum_tensor("psa", [128, 1024], F32))
        PSB = alloc(nc.psum_tensor("psb", [128, 1024], F32))
        PSW0 = alloc(nc.psum_tensor("psw0", [128, 1024], F32))
        PSW1 = alloc(nc.psum_tensor("psw1", [128, 1024], F32))
        PS0 = PSA[:, 0:512]
        PS1 = PSA[:, 512:1024]
        PS6 = PSB[:, 0:512]
        PS7 = PSB[:, 512:1024]
        PS = [PS0, PS1, None, None, None, None, PS6, PS7]
        PSW = [PSW0, PSW1]
        SUMAP = [PSW0[0:1, 0:512], PSW0[0:1, 512:1024],
                 PSW1[0:1, 0:512], PSW1[0:1, 512:1024]]
        SQAP = [PS0[0:1, :], PS1[0:1, :], PS6[0:1, :], PS7[0:1, :]]
        W2P = [PSW0[:, 0:512], PSW0[:, 512:1024]]
        PTB = [PSW1[:].bitcast(BF16)[:, 0:1024], PSW1[:].bitcast(BF16)[:, 1024:2048]]

        WB = [alloc(nc.sbuf_tensor(f"sb_w{i}", [128, NCH * 1024], BF16)) for i in range(3)]
        sb_lnx = alloc(nc.sbuf_tensor("sb_lnx", [128, NCH * T], BF16))

        def wchunk(ap):
            return ap.rearrange("(c p) o -> c p o", p=128).transpose([1, 0, 2])

        def sb3(t, inner):
            return t[:].rearrange("p (c o) -> p c o", o=inner)
        def dma(queue, out, in_, sem, waits=()):
            ins = None
            if em.act(queue, waits):
                ins = em.eng.dma_start(out=out, in_=in_)
            return em.fin(ins, sem, 16)

        BROW = {"s_db0": 0, "s_db1": 1, "s_db2": 2, "s_db3": 3}

        def bcast(row_ap, out_ap, sem, waits=(), queue="sync"):
            n_part = out_ap.shape[0]
            n = row_ap.shape[-1]
            row = BROW[sem]
            scr = bscr if row_ap.dtype == F32 else bscrb
            ins = None
            if em.act(queue, list(waits) + [(sem, em.now(sem))]):
                ins = em.eng.dma_start(out=scr[row:row + 1, 0:n], in_=row_ap)
            em.fin(ins, "s_dbw", 16)
            ins = None
            if em.act(queue, [("s_dbw", em.now("s_dbw"))]):
                ins = em.eng.dma_start(
                    out=out_ap, in_=scr[row:row + 1, 0:n].partition_broadcast(n_part))
            return em.fin(ins, sem, 16)

        def mme(out, lhsT, rhs, start, stop, waits=(), tp=None, inc=False):
            ins = None
            if em.act("tensor", waits):
                ins = em.eng.matmul(out, lhsT, rhs, start=start, stop=stop,
                                    tile_position=tp, skip_group_check=True)
            if inc:
                return em.fin(ins, "s_mm", 1)
            return None

        def trans(out, in_, waits=()):
            ins = None
            if em.act("tensor", waits):
                ins = em.eng.transpose(out, in_, sb_id[:])
            return em.fin(ins, "s_mm", 1)

        def ve(f, waits=()):
            ins = None
            if em.act("vector", waits):
                ins = f(em.eng)
            return em.fin(ins, "s_ve", 1)

        def sc(f, waits=()):
            ins = None
            if em.act("scalar", waits):
                ins = f(em.eng)
            return em.fin(ins, "s_sc", 1)

        def ln_rows(R0, R1, R2, n_tok, sum_srcs, sq_srcs, mrep, rrep, bc0, bc1):
            wl = [("s_mm", em.now("s_mm"))]
            for (ap_, c0, w) in sum_srcs:
                ve(lambda e, a=ap_, c=c0, ww=w: e.tensor_copy(R0[0:1, c:c + ww], a), waits=wl)
            for (ap_, c0, w) in sq_srcs:
                ve(lambda e, a=ap_, c=c0, ww=w: e.tensor_copy(R1[0:1, c:c + ww], a))
            ve(lambda e: e.tensor_scalar_mul(R0[0:1, :n_tok], R0[0:1, :n_tok], 1.0 / D))
            ve(lambda e: e.tensor_mul(R2[0:1, :n_tok], R0[0:1, :n_tok], R0[0:1, :n_tok]))
            ve(lambda e: e.scalar_tensor_tensor(R1[0:1, :n_tok], R1[0:1, :n_tok], 1.0 / D,
                                                R2[0:1, :n_tok], ALU.mult, ALU.subtract))
            sc(lambda e: e.activation(out=R1[0:1, :n_tok], in_=R1[0:1, :n_tok],
                                      func=ACT.Ln, bias=sb_eps[0:1, 0:1]),
               waits=[("s_ve", em.now("s_ve")), ("s_ve", eps_ve[0])])
            sc(lambda e: e.activation(out=R2[0:1, :n_tok], in_=R1[0:1, :n_tok],
                                      func=ACT.Exp, scale=-0.5))
            vnow = em.now("s_ve")
            snow = em.now("s_sc")
            b0 = bcast(R0[0:1, :n_tok], mrep[:, :n_tok], bc0, waits=[("s_ve", vnow)])
            b1 = bcast(R2[0:1, :n_tok], rrep[:, :n_tok], bc1, waits=[("s_sc", snow)],
                       queue="gpsimd")
            return b0, b1

        ph1 = ExitStack()
        sb_x1 = ph1.enter_context(nc.sbuf_tensor("sb_x1", [128, NCH * T], F32))
        sb_xbc = [ph1.enter_context(nc.sbuf_tensor("sb_xbc0", [128, T], BF16))] * 2
        sb_sqc = [ph1.enter_context(nc.sbuf_tensor("sb_sqc0", [128, T], BF16))] * 2
        sb_R0 = ph1.enter_context(nc.sbuf_tensor("sb_R0", [1, T], F32))
        sb_R1 = ph1.enter_context(nc.sbuf_tensor("sb_R1", [1, T], F32))
        sb_R2 = ph1.enter_context(nc.sbuf_tensor("sb_R2", [1, T], F32))
        sb_mrep = ph1.enter_context(nc.sbuf_tensor("sb_mrep", [128, T], F32))
        sb_rrep = ph1.enter_context(nc.sbuf_tensor("sb_rrep", [128, T], F32))

        xTc = xT.rearrange("(c p) t -> c p t", p=128)

        eps_ve = [0]

        def phase1():
            eps_ve[0] = ve(lambda e: e.memset(sb_eps[:], EPS))
            for (dst, src) in [(sb_bq, bqc), (sb_bk, bkc), (sb_bv, bvc), (sb_bo, boc),
                               (sb_sb1, sb1c), (sb_sb2, sb2c), (sb_eb1, eb1r),
                               (sb_rb, rbrep), (sb_ones, onesb), (sb_ind, indb),
                               (sb_id, identb), (sb_id64, id64db)]:
                dma("sync", dst[:], src, "s_dinit")
            dma("gpsimd", sb3(sb_rw, E), wchunk(rw), "s_dcw0")
            dma("gpsimd", sb_eb2[:], eb2n, "s_dcw1")
            for c in range(NCH):
                sx = "s_dxc0" if c % 2 == 0 else "s_dxc1"
                dma("sync", sb_x1[:, c * T:(c + 1) * T], xTc[c], sx)
            mm_after = {}
            for c in range(NCH):
                sl = c % 2
                sxc = "s_dxc0" if c % 2 == 0 else "s_dxc1"
                sc(lambda e, s=sl, cc=c: e.copy(sb_xbc[s][:], sb_x1[:, cc * T:(cc + 1) * T]),
                   waits=[(sxc, 16 * (c // 2 + 1)), ("s_mm", mm_after.get(c - 1, 0))])
                sc(lambda e, s=sl: e.activation(out=sb_sqc[s][:], in_=sb_xbc[s][:],
                                                func=ACT.Square))
                scn = em.now("s_sc")
                for qc in range(NQB):
                    mme(SUMAP[qc], sb_ones[:], sb_xbc[sl][:, qc * 512:(qc + 1) * 512],
                        start=(c == 0), stop=(c == NCH - 1),
                        waits=[("s_sc", scn)], inc=(c == NCH - 1))
                for qc in range(NQB):
                    mme(SQAP[qc], sb_ones[:], sb_sqc[sl][:, qc * 512:(qc + 1) * 512],
                        start=(c == 0), stop=(c == NCH - 1),
                        inc=(c == NCH - 1 or qc == NQB - 1))
                mm_after[c] = em.now("s_mm")
            ln_rows(sb_R0, sb_R1, sb_R2, T,
                    [(SUMAP[qc], qc * 512, 512) for qc in range(NQB)],
                    [(SQAP[qc], qc * 512, 512) for qc in range(NQB)],
                    sb_mrep, sb_rrep, "s_db0", "s_db1")
            for c in range(NCH):
                wl = [("s_db0", em.now("s_db0")), ("s_db1", em.now("s_db1"))]
                ve(lambda e, cc=c: e.tensor_sub(sb_x1[:, cc * T:(cc + 1) * T],
                                                sb_x1[:, cc * T:(cc + 1) * T],
                                                sb_mrep[:]), waits=wl)
                ve(lambda e, cc=c: e.tensor_mul(sb_lnx[:, cc * T:(cc + 1) * T],
                                                sb_x1[:, cc * T:(cc + 1) * T], sb_rrep[:]))
            if DEBUG_DUMPS:
                wl = [("s_ve", em.now("s_ve")), ("s_sc", em.now("s_sc")),
                      ("s_db0", em.now("s_db0")), ("s_db1", em.now("s_db1"))]
                dma("gpsimd", dumps["d_ln1"], sb_lnx[:], "s_dst", waits=wl)
                dma("sync", dumps["d_R0"], sb_R0[0:1, :], "s_dst", waits=wl)
                dma("sync", dumps["d_R1"], sb_R1[0:1, :], "s_dst")
                dma("sync", dumps["d_R2"], sb_R2[0:1, :], "s_dst")
                dma("sync", dumps["d_mr1"], sb_mrep[:], "s_dst")
                dma("sync", dumps["d_rr1"], sb_rrep[:], "s_dst")

        em.run_phase("ln1", phase1)
        ph1.close()

        phA = ExitStack()
        sb_ao = phA.enter_context(nc.sbuf_tensor("sb_ao", [128, NCH * T], BF16))
        ph2 = ExitStack()
        sb_g = [ph2.enter_context(nc.sbuf_tensor(f"sb_g{i}", [128, 2 * GW], BF16)) for i in range(2)]
        sb_q = ph2.enter_context(nc.sbuf_tensor("sb_q", [128, T], BF16))
        sb_k = ph2.enter_context(nc.sbuf_tensor("sb_k", [128, T], BF16))
        sb_vf = ph2.enter_context(nc.sbuf_tensor("sb_vf", [128, T], BF16))
        sb_va = ph2.enter_context(nc.sbuf_tensor("sb_va", [128, 16 * 130], BF16))
        sb_pt2 = ph2.enter_context(nc.sbuf_tensor("sb_pt2", [128, 16 * 2 * QB], BF16))
        sb_rr = ph2.enter_context(nc.sbuf_tensor("sb_rr", [1, 2 * QB], F32))
        sb_cc = ph2.enter_context(nc.sbuf_tensor("sb_cc", [2, 16], F32))
        AVBANK = {0: 0, 1: 1, 2: 0, 3: 0}


        def attention():
            em.guard_all()
            st = {
                "pair_mm": em.now("s_mm"),
                "pair_ve": em.now("s_ve"),
                "pair_sc": em.now("s_sc"),
                "g_rdy": {},
                "g_war": {},
                "qb_norm_ve": {},
                "av_done_mm": 0,
                "tt_done": {},
            }
            for i, w in enumerate([wq, wk, wv]):
                dma("gpsimd", sb3(WB[i], 1024), wchunk(w), f"s_dw{i}")
            sc(lambda e: e.mul(WB[0][:], WB[0][:], 0.125), waits=[("s_dw0", 16)])
            sc(lambda e: e.mul(sb_bq8[:], sb_bq[:], 0.125),
               waits=[("s_dinit", em.now("s_dinit"))])
            st["wqs"] = em.now("s_sc")
            va3 = sb_va[:].rearrange("p (t c) -> p t c", c=130)
            ve(lambda e: e.memset(va3[:, :, 64:65], 1.0))
            ve(lambda e: e.memset(va3[:, :, 129:130], 1.0))
            st["va_ones"] = em.now("s_ve")
            ve(lambda e: e.memset(sb_e1[0:1, :], 1.0))
            st["e1_ve"] = em.now("s_ve")
            st["g_dma"] = {}

            def load_g(p_, war_ve):
                slot = p_ % 2
                dma("gpsimd", sb_g[slot][:].rearrange("p (h w) -> p h w", w=GW),
                    gH[2 * p_: 2 * p_ + 2].transpose([1, 0, 2]), f"s_dg{slot}",
                    waits=[("s_ve", war_ve)])
                st["g_dma"][slot] = em.now(f"s_dg{slot}")

            def exp_g(p_):
                slot = p_ % 2
                sc(lambda e, sl=slot: e.activation(out=sb_g[sl][:], in_=sb_g[sl][:],
                                                   func=ACT.Exp),
                   waits=[(f"s_dg{slot}", st["g_dma"][slot])])
                st["g_rdy"][slot] = em.now("s_sc")

            load_g(0, 0)
            exp_g(0)
            st["load_g"] = load_g
            st["exp_g"] = exp_g
            for p in range(NCH):
                pair_prologue(p, st)
                av_prev = None
                for qb in range(NQB):
                    av_prev = qb_scores(p, qb, st, av_prev)
                qb_av_tail(p, NQB - 1, st)
                ve(lambda e, pp=p: e.tensor_scalar_add(
                    sb_ao[:, pp * T:(pp + 1) * T], sb_ao[:, pp * T:(pp + 1) * T],
                    sb_bv[:, pp:pp + 1]))
                st["pair_mm"] = em.now("s_mm")
                st["pair_ve"] = em.now("s_ve")
                st["pair_sc"] = em.now("s_sc")
                st["g_war"][p % 2] = em.now("s_ve")
            if DEBUG_DUMPS:
                wl = [("s_sc", em.now("s_sc")), ("s_ve", em.now("s_ve"))]
                dma("gpsimd", dumps["d_attno"], sb_ao[:], "s_dst", waits=wl)
                dma("gpsimd", dumps["d_q"], sb_q[:], "s_dst", waits=wl)
                dma("gpsimd", dumps["d_k"], sb_k[:], "s_dst", waits=wl)
                dma("gpsimd", dumps["d_va"], sb_va[:], "s_dst", waits=wl)
                dma("gpsimd", dumps["d_pt0"], sb_pt2[:, 0:16 * QB], "s_dst", waits=wl)
                dma("sync", dumps["d_negc"], sb_negc[:], "s_dst", waits=wl)

        def pair_prologue(p, st):
            pv, pm, psc = st["pair_ve"], st["pair_mm"], st["pair_sc"]
            ev = {}
            for wi, (wsem, bias, dest) in enumerate([("s_dw0", sb_bq8, sb_q),
                                                     ("s_dw1", sb_bk, sb_k)]):
                for qb in range(NQB):
                    ps = PS[6] if qb % 2 == 0 else PS[7]
                    wl = [(wsem, 16), ("s_ve", st["qb_norm_ve"].get(1, 0)),
                          ("s_sc", st["wqs"])]
                    key = (wi, qb)
                    if (wi, qb) not in [(0, 0), (0, 1)]:
                        pk = (wi, qb - 2) if qb >= 2 else (wi - 1, qb + 2)
                        if pk in ev:
                            wl.append(("s_ve", ev[pk]))
                    for dc in range(NCH):
                        mme(ps[:], WB[wi][:, dc * 1024 + p * 128: dc * 1024 + p * 128 + 128],
                            sb_lnx[:, dc * T + qb * 512: dc * T + qb * 512 + 512],
                            start=(dc == 0), stop=(dc == NCH - 1),
                            waits=wl if dc == 0 else (), inc=(dc == NCH - 1))
                    ev[key] = ve(lambda e, pp=ps, d=dest, q=qb, b=bias:
                                 e.tensor_scalar_add(d[:, q * 512:(q + 1) * 512],
                                                     pp[:], b[:, p:p + 1]),
                                 waits=[("s_mm", em.now("s_mm")), ("s_mm", pm)])
            evv = {}
            for qb in range(NQB):
                ps = PS[6] if qb % 2 == 0 else PS[7]
                wl = [("s_dw2", 16)]
                if qb >= 2:
                    wl.append(("s_ve", evv[qb - 2]))
                else:
                    wl.append(("s_ve", ev[(1, qb + 2)]))
                for dc in range(NCH):
                    mme(ps[:], WB[2][:, dc * 1024 + p * 128: dc * 1024 + p * 128 + 128],
                        sb_lnx[:, dc * T + qb * 512: dc * T + qb * 512 + 512],
                        start=(dc == 0), stop=(dc == NCH - 1),
                        waits=wl if dc == 0 else (), inc=(dc == NCH - 1))
                evv[qb] = ve(lambda e, pp=ps, q=qb:
                             e.tensor_copy(sb_vf[:, q * 512:(q + 1) * 512], pp[:]),
                             waits=[("s_mm", em.now("s_mm")), ("s_mm", pm)])
            tev = {}
            for tc in range(16):
                pb = PTB[tc % 2]
                wl = [("s_ve", evv[tc // 4]), ("s_sc", psc)]
                if tc >= 2:
                    wl.append(tev[tc - 2])
                trans(pb[:, 0:128], sb_vf[:, tc * 128:(tc + 1) * 128], waits=wl)
                ewl = [("s_mm", em.now("s_mm")), ("s_mm", pm),
                       ("s_ve", st["va_ones"])]
                ve(lambda e, b=pb, t=tc:
                   e.tensor_copy(sb_va[:, t * 130: t * 130 + 64], b[:, 0:64]),
                   waits=ewl)
                tev[tc] = ve(lambda e, b=pb, t=tc:
                             e.tensor_copy(sb_va[:, t * 130 + 65: t * 130 + 129],
                                           b[:, 64:128]))
                tev[tc] = ("s_ve", tev[tc])
            tr_done = em.now("s_mm")
            ncm = {}
            rdone = {}
            rrg = [PS[6][0:1, :], PS[7][0:1, :], SUMAP[0], SUMAP[1]]
            for half in range(2):
                tt_cnt = {}
                for k in range(4):
                    j = half * 4 + k
                    i, qc = divmod(j, NQB)
                    src = sb_q if i == 0 else sb_k
                    sq_sl = sb_vf[:, k * 512: k * 512 + 512]
                    swl = [("s_mm", ncm.get(j - 4, tr_done))]
                    tt_cnt[j] = ve(lambda e, s=src, q=qc, o=sq_sl:
                                   e.tensor_mul(o, s[:, q * 512:(q + 1) * 512],
                                                s[:, q * 512:(q + 1) * 512]),
                                   waits=swl)
                for k in range(4):
                    j = half * 4 + k
                    mwl = [("s_ve", tt_cnt[j]), ("s_dinit", em.now("s_dinit")),
                           ("s_sc", psc)]
                    if j - 4 in rdone:
                        mwl.append(("s_ve", rdone[j - 4]))
                    mme(rrg[k], sb_ones[:], sb_vf[:, k * 512: k * 512 + 512],
                        start=True, stop=True, waits=mwl, inc=True)
                    ncm[j] = em.now("s_mm")
                for k in range(4):
                    j = half * 4 + k
                    rdone[j] = ve(lambda e, pp=rrg[k], col=j:
                                  e.reduce_max(sb_cc[0:1, col:col + 1], pp, axis=AX),
                                  waits=[("s_mm", ncm[j])])
            ve(lambda e: e.reduce_max(sb_cc[0:1, 8:9], sb_cc[0:1, 0:4], axis=AX))
            ve(lambda e: e.memset(sb_cc[0:1, 15:16], 0.0))
            ve(lambda e: e.reduce_max(sb_cc[0:1, 9:10], sb_cc[0:1, 4:8], axis=AX))
            sc(lambda e: e.activation(out=sb_cc[0:1, 10:11], in_=sb_cc[0:1, 8:9],
                                      func=ACT.Sqrt),
               waits=[("s_ve", em.now("s_ve"))])
            sc(lambda e: e.activation(out=sb_cc[0:1, 11:12], in_=sb_cc[0:1, 9:10],
                                      func=ACT.Sqrt))
            ve(lambda e: e.tensor_mul(sb_cc[0:1, 12:13], sb_cc[0:1, 10:11],
                                      sb_cc[0:1, 11:12]),
               waits=[("s_sc", em.now("s_sc"))])
            sc(lambda e: e.mul(sb_cc[0:1, 14:15], sb_cc[0:1, 12:13], -1.0),
               waits=[("s_ve", em.now("s_ve"))])
            par = p % 2
            wl = [("s_sc", em.now("s_sc")), ("s_sc", psc)]
            bcast(sb_cc[0:1, 14:15], sb_negc[:, par: par + 1], "s_db2", waits=wl)
            st["negc"] = em.now("s_db2")
            st["prologue_sc"] = em.now("s_sc")
            st["prologue_ve"] = em.now("s_ve")
            if p + 1 < NCH:
                load_g_fn = st["load_g"]
                load_g_fn(p + 1, st["g_war"].get((p + 1) % 2, 0))

        def qb_scores(p, qb, st, av_prev):
            par = p % 2
            nkc = 4 * qb + 4
            exp_done = {}
            pnkc = av_prev[1] if av_prev else 0
            for kc in range(nkc):
                psW = PSW[kc % 2]
                if kc - 2 in exp_done:
                    wv = exp_done[kc - 2]
                else:
                    tail = st.get("qb_exp_tail")
                    if tail is not None:
                        wv = max(tail[kc], st["prologue_sc"]) if qb == 0 \
                            else tail[kc]
                    else:
                        wv = max(st.get("qb_exp_last", 0), st["prologue_sc"])
                wl = [("s_sc", wv)]
                if qb == 0 and kc < 2:
                    wl.append(("s_ve", st["prologue_ve"]))
                goff = 512 * qb - 128 * kc + 384
                gsl = sb_g[p % 2][:].rearrange("p (h w) -> p h w", w=GW)
                mme(psW[:, 0:512], sb_k[0:64, kc * 128: kc * 128 + 128],
                    sb_q[0:64, qb * 512:(qb + 1) * 512],
                    start=True, stop=True, tp=(0, 0), waits=wl)
                mme(psW[:, 512:1024], sb_k[64:128, kc * 128: kc * 128 + 128],
                    sb_q[64:128, qb * 512:(qb + 1) * 512],
                    start=True, stop=True, tp=(64, 0), inc=True)
                if av_prev and kc < pnkc:
                    qprev, _ = av_prev
                    bank = AVBANK[qprev]
                    for h in range(2):
                        psA = PS[h] if bank == 0 else PS[6 + h]
                        awl = [("s_ve", st["tt_done_prev"][kc])]
                        if kc == 0:
                            awl.append(("s_ve", st["qb_norm_ve"].get(bank, 0)))
                        mme(psA[0:65, :],
                            sb_va[:, kc * 130 + 65 * h: kc * 130 + 65 * h + 65],
                            sb_pt2[:, kc * 1024 + h * 512: kc * 1024 + h * 512 + 512],
                            start=(kc == 0), stop=(kc == pnkc - 1),
                            waits=awl, inc=(kc == pnkc - 1))
                if av_prev and kc == pnkc - 1:
                    qb_norm(p, av_prev[0], st)
                mm_done = em.now("s_mm")
                ewl = [("s_mm", mm_done), ("s_db2", st["negc"]),
                       ("s_ve", st["tt_done"].get(("war", kc), 0))]
                sc(lambda e, pw=psW, k=kc:
                   e.activation(out=sb_pt2[:, k * 1024: k * 1024 + 1024],
                                in_=pw[:], func=ACT.Exp,
                                bias=sb_negc[:, par: par + 1]),
                   waits=ewl)
                exp_done[kc] = em.now("s_sc")
                pt3 = sb_pt2[:, kc * 1024: kc * 1024 + 1024].rearrange(
                    "p (h q) -> p h q", q=512)
                ve(lambda e, p3=pt3, gg=gsl, o=goff:
                   e.tensor_mul(p3, p3, gg[:, :, o: o + 512]),
                   waits=[("s_sc", exp_done[kc]),
                          (f"s_dg{p % 2}", st["g_rdy"][p % 2] if False else 0),
                          ("s_sc", st["g_rdy"][p % 2])])
                st["tt_done"][kc] = em.now("s_ve")
            st["qb_exp_last"] = em.now("s_sc")
            st["qb_exp_tail"] = (exp_done[nkc - 2], exp_done[nkc - 1])
            st["tt_done_prev"] = dict(st["tt_done"])
            return (qb, nkc)

        def qb_av_tail(p, qb, st):
            nkc = 4 * qb + 4
            bank = AVBANK[qb]
            tt_done = st["tt_done_prev"]
            for h in range(2):
                psA = PS[h] if bank == 0 else PS[6 + h]
                for kc in range(nkc):
                    wl = [("s_ve", tt_done[kc])]
                    if kc == 0:
                        wl.append(("s_ve", st["qb_norm_ve"].get(bank, 0)))
                    mme(psA[0:65, :],
                        sb_va[:, kc * 130 + 65 * h: kc * 130 + 65 * h + 65],
                        sb_pt2[:, kc * 1024 + h * 512: kc * 1024 + h * 512 + 512],
                        start=(kc == 0), stop=(kc == nkc - 1),
                        waits=wl, inc=(kc == nkc - 1))
            if p + 1 < NCH:
                st["exp_g"](p + 1)
            qb_norm(p, qb, st)

        def qb_norm(p, qb, st):
            bank = AVBANK[qb]
            PSab = PSA if bank == 0 else PSB
            av_mm = em.now("s_mm")
            buf = qb % 2
            rbuf = sb_rr[0:1, :] if buf == 0 else sb_rr2[0:1, :]
            sc(lambda e, pa=PSab: e.activation(out=sb_rln[0:1, :], in_=pa[64:65, 0:1024],
                                               func=ACT.Ln),
               waits=[("s_mm", av_mm)])
            sc(lambda e, rb=rbuf: e.activation(out=rb, in_=sb_rln[0:1, :],
                                               func=ACT.Exp, scale=-1.0),
               waits=[("s_dbw", st.get("rrw", 0))])
            bs = "s_db0" if buf == 0 else "s_db1"
            bcast(rbuf, sb_r64c[buf][:, :], bs, waits=[("s_sc", em.now("s_sc"))])
            st["rrw"] = em.now("s_dbw")
            bn = em.now(bs)
            for h in range(2):
                dsl = sb_ao[64 * h: 64 * h + 64, p * T + qb * 512: p * T + qb * 512 + 512]
                ve(lambda e, pa=PSab, d=dsl, hh=h, bf=buf:
                   e.tensor_mul(d, pa[0:64, hh * 512:(hh + 1) * 512],
                                sb_r64c[bf][:, hh * 512:(hh + 1) * 512]),
                   waits=[(bs, bn)])
            st["qb_norm_ve"][bank] = em.now("s_ve")

        sb_rr2 = ph2.enter_context(nc.sbuf_tensor("sb_rr2", [1, 2 * QB], F32))
        sb_rln = ph2.enter_context(nc.sbuf_tensor("sb_rln", [1, 2 * QB], F32))
        sb_e1 = ph2.enter_context(nc.sbuf_tensor("sb_e1", [1, 64], F32))
        sb_r64c = [ph2.enter_context(nc.sbuf_tensor(f"sb_r64c{i}", [64, 2 * QB], F32))
                   for i in range(2)]

        if PHASE_LIMIT >= 2:
            em.run_phase("attn", attention)
        ph2.close()

        ph3 = ExitStack()
        sb_R0b = ph3.enter_context(nc.sbuf_tensor("sb_R0b", [1, T], F32))
        sb_R1b = ph3.enter_context(nc.sbuf_tensor("sb_R1b", [1, T], F32))
        sb_R2b = ph3.enter_context(nc.sbuf_tensor("sb_R2b", [1, T], F32))
        sb_mrep3 = ph3.enter_context(nc.sbuf_tensor("sb_mrep3", [128, T], F32))
        sb_rrep3 = ph3.enter_context(nc.sbuf_tensor("sb_rrep3", [128, T], F32))
        sb_x3 = [ph3.enter_context(nc.sbuf_tensor(f"sb_x3{i}", [128, T], F32)) for i in range(2)]
        sb_x23 = [ph3.enter_context(nc.sbuf_tensor(f"sb_x23{i}", [128, T], F32)) for i in range(2)]
        sb_sq3 = ph3.enter_context(nc.sbuf_tensor("sb_sq3", [128, T], BF16))
        sb_xb3 = ph3.enter_context(nc.sbuf_tensor("sb_xb3", [128, T], BF16))

        def phase3():
            em.guard_all()
            attn_ve = em.now("s_ve")
            dma("gpsimd", sb3(WB[0], 1024), wchunk(wo), "s_dw0",
                waits=[("s_mm", em.now("s_mm"))])
            ev = {}
            for oc in range(NCH):
                for tb in range(NQB):
                    ps = PS[0] if (oc * NQB + tb) % 2 == 0 else PS[1]
                    wl = [("s_dw0", em.now("s_dw0")), ("s_ve", attn_ve)]
                    if (oc * NQB + tb) >= 2:
                        wl.append(("s_sc", ev[(oc * NQB + tb) - 2]))
                    for fc in range(NCH):
                        mme(ps[:], WB[0][:, fc * 1024 + oc * 128: fc * 1024 + oc * 128 + 128],
                            sb_ao[:, fc * T + tb * 512: fc * T + tb * 512 + 512],
                            start=(fc == 0), stop=(fc == NCH - 1),
                            waits=wl if fc == 0 else (), inc=(fc == NCH - 1))
                    ev[oc * NQB + tb] = sc(
                        lambda e, pp=ps, o=oc, t=tb:
                        e.add(sb_lnx[:, o * T + t * 512: o * T + t * 512 + 512],
                              pp[:], sb_bo[:, o:o + 1]),
                        waits=[("s_mm", em.now("s_mm"))])

            stat_mm = {}
            for c in range(NCH):
                swl = [("s_sc", ev[c * NQB + NQB - 1]), ("s_mm", stat_mm.get(c - 1, 0))]
                sc(lambda e, cc=c: e.activation(out=sb_sq3[:],
                                                in_=sb_lnx[:, cc * T:(cc + 1) * T],
                                                func=ACT.Square), waits=swl)
                scn = em.now("s_sc")
                for qc in range(NQB):
                    wl2 = [("s_sc", scn)]
                    if c == 0:
                        wl2.append(("s_ve", em.now("s_ve")))
                    mme(SUMAP[qc], sb_ones[:],
                        sb_lnx[:, c * T + qc * 512: c * T + qc * 512 + 512],
                        start=(c == 0), stop=(c == NCH - 1),
                        waits=wl2, inc=(c == NCH - 1))
                for qc in range(NQB):
                    mme(SQAP[qc], sb_ones[:], sb_sq3[:, qc * 512:(qc + 1) * 512],
                        start=(c == 0), stop=(c == NCH - 1),
                        inc=(c == NCH - 1 or qc == NQB - 1))
                stat_mm[c] = em.now("s_mm")
            ln_rows(sb_R0b, sb_R1b, sb_R2b, T,
                    [(SUMAP[qc], qc * 512, 512) for qc in range(NQB)],
                    [(SQAP[qc], qc * 512, 512) for qc in range(NQB)],
                    sb_mrep3, sb_rrep3, "s_db0", "s_db1")
            if DEBUG_DUMPS:
                dma("sync", dumps["d_m3"], sb_R0b[0:1, :], "s_dst",
                    waits=[("s_ve", em.now("s_ve")), ("s_sc", em.now("s_sc"))])
                dma("sync", dumps["d_r3"], sb_R2b[0:1, :], "s_dst")

            x2w = {}
            stat2_mm = {}
            tt_done = {}
            for c in range(NCH):
                sl = c % 2
                sxx = "s_dxc0" if sl == 0 else "s_dxc1"
                x2sl = sb_x23[sl][:]
                dma("sync", sb_x3[sl][:], xTc[c], sxx,
                    waits=[("s_ve", tt_done.get(c - 2, 0))])
                wl = [(sxx, em.now(sxx)), ("s_db0", em.now("s_db0")),
                      ("s_db1", em.now("s_db1")), ("s_dx2w", x2w.get(c - 2, 0)),
                      ("s_mm", em.now("s_mm"))]
                ve(lambda e, cc=c, x2b=x2sl: e.tensor_sub(
                    x2b, sb_lnx[:, cc * T:(cc + 1) * T], sb_mrep3[:]), waits=wl)
                ve(lambda e, x2b=x2sl: e.tensor_mul(x2b, x2b, sb_rrep3[:]))
                ve(lambda e, s=sl, x2b=x2sl: e.tensor_add(x2b, x2b, sb_x3[s][:]))
                tt_done[c] = em.now("s_ve")
                x2w[c] = dma("gpsimd", x2s[c], x2sl, "s_dx2w",
                             waits=[("s_ve", em.now("s_ve"))])
                sc(lambda e, x2b=x2sl: e.copy(sb_xb3[:], x2b),
                   waits=[("s_ve", em.now("s_ve")), ("s_mm", stat2_mm.get(c - 1, 0))])
                sc(lambda e: e.activation(out=sb_sq3[:], in_=sb_xb3[:], func=ACT.Square))
                scn = em.now("s_sc")
                for qc in range(NQB):
                    wl2 = [("s_sc", scn)]
                    if c == 0:
                        wl2.append(("s_ve", em.now("s_ve")))
                    mme(SUMAP[qc], sb_ones[:], sb_xb3[:, qc * 512:(qc + 1) * 512],
                        start=(c == 0), stop=(c == NCH - 1),
                        waits=wl2, inc=(c == NCH - 1))
                for qc in range(NQB):
                    mme(SQAP[qc], sb_ones[:], sb_sq3[:, qc * 512:(qc + 1) * 512],
                        start=(c == 0), stop=(c == NCH - 1),
                        inc=(c == NCH - 1 or qc == NQB - 1))
                stat2_mm[c] = em.now("s_mm")
            ln_rows(sb_R0b, sb_R1b, sb_R2b, T,
                    [(SUMAP[qc], qc * 512, 512) for qc in range(NQB)],
                    [(SQAP[qc], qc * 512, 512) for qc in range(NQB)],
                    sb_mrep3, sb_rrep3, "s_db2", "s_db3")
            if DEBUG_DUMPS:
                dma("sync", dumps["d_m3b"], sb_R0b[0:1, :], "s_dst",
                    waits=[("s_ve", em.now("s_ve")), ("s_sc", em.now("s_sc"))])
                dma("sync", dumps["d_r3b"], sb_R2b[0:1, :], "s_dst")
            app_tt = {}
            for c in range(NCH):
                sl = c % 2
                sxx = "s_dxc0" if sl == 0 else "s_dxc1"
                dma("sync", sb_x3[sl][:], x2s[c], sxx,
                    waits=[("s_ve", app_tt.get(c - 2, 0)), ("s_dx2w", x2w[c])])
                wl = [(sxx, em.now(sxx)), ("s_db2", em.now("s_db2")),
                      ("s_db3", em.now("s_db3"))]
                ve(lambda e, s=sl: e.tensor_sub(sb_x23[0][:], sb_x3[s][:], sb_mrep3[:]),
                   waits=wl)
                app_tt[c] = ve(lambda e, cc=c: e.tensor_mul(
                    sb_lnx[:, cc * T:(cc + 1) * T], sb_x23[0][:], sb_rrep3[:]))
            if DEBUG_DUMPS:
                dma("gpsimd", dumps["d_ln2"], sb_lnx[:], "s_dst",
                    waits=[("s_ve", em.now("s_ve"))])

        if PHASE_LIMIT >= 3:
            em.run_phase("wo_ln2", phase3)
        ph3.close()
        phA.close()

        ph4 = ExitStack()
        sb_lgw = ph4.enter_context(nc.sbuf_tensor("sb_lgw", [128, 16 * 15], F32))
        sb_exw = ph4.enter_context(nc.sbuf_tensor("sb_exw", [128, 16 * 15], F32))
        sb_cww = ph4.enter_context(nc.sbuf_tensor("sb_cww", [128, 16 * 15], F32))
        sb_t8w = ph4.enter_context(nc.sbuf_tensor("sb_t8w", [128, 16 * 8], F32))
        sb_ssm = ph4.enter_context(nc.sbuf_tensor("sb_ssm", [128, 16], F32))
        sb_sss = ph4.enter_context(nc.sbuf_tensor("sb_sss", [128, 16], F32))
        sb_ssr = ph4.enter_context(nc.sbuf_tensor("sb_ssr", [128, 16], F32))
        sb_cwp = ph4.enter_context(nc.sbuf_tensor("sb_cwp", [128, 128], BF16))

        def phase4():
            em.guard_all()
            ve(lambda e: e.memset(sb_cwp[:], 0.0))
            add_ve = {}
            for tt in range(16):
                ps = PS[6] if tt % 2 == 0 else PS[7]
                wl = [("s_dcw0", em.now("s_dcw0")), ("s_ve", add_ve.get(tt - 2, 0)),
                      ("s_ve", em.now("s_ve") if tt == 0 else 0)]
                for dc in range(NCH):
                    mme(ps[0:128, 0:15],
                        sb_lnx[:, dc * T + tt * 128: dc * T + tt * 128 + 128],
                        sb_rw[:, dc * E: dc * E + E],
                        start=(dc == 0), stop=(dc == NCH - 1),
                        waits=wl if dc == 0 else (), inc=(dc == NCH - 1))
                add_ve[tt] = ve(
                    lambda e, pp=ps, t=tt: e.tensor_add(
                        sb_lgw[:, t * 15: t * 15 + 15], pp[0:128, 0:15], sb_rb[:, :]),
                    waits=[("s_mm", em.now("s_mm")), ("s_dinit", em.now("s_dinit"))])
            for tt in range(16):
                ve(lambda e, t=tt: e.max(out=sb_t8w[:, t * 8: t * 8 + 8],
                                         in_=sb_lgw[:, t * 15: t * 15 + 15]))
            lg3 = sb_lgw[:].rearrange("p (t e) -> p t e", e=15)
            ve(lambda e: e.tensor_reduce(sb_ssm[:].rearrange("p (t o) -> p t o", o=1),
                                         lg3, AX, ALU.max, negate=True))
            for tt in range(16):
                sc(lambda e, t=tt: e.activation(
                    out=sb_exw[:, t * 15: t * 15 + 15],
                    in_=sb_lgw[:, t * 15: t * 15 + 15], func=ACT.Exp,
                    bias=sb_ssm[:, t: t + 1], accum_out=sb_sss[:, t: t + 1]),
                   waits=[("s_ve", em.now("s_ve"))] if tt == 0 else ())
            ve(lambda e: e.reciprocal(sb_ssr[:], sb_sss[:]),
               waits=[("s_sc", em.now("s_sc"))])
            for tt in range(16):
                ve(lambda e, t=tt: e.tensor_single_scalar(
                    sb_lgw[:, t * 15: t * 15 + 15], sb_lgw[:, t * 15: t * 15 + 15],
                    sb_t8w[:, t * 8 + 2: t * 8 + 3], ALU.is_ge))
            ve(lambda e: e.tensor_mul(sb_exw[:], sb_exw[:], sb_lgw[:]))
            for tt in range(16):
                ve(lambda e, t=tt: e.tensor_scalar_mul(
                    sb_cww[:, t * 15: t * 15 + 15], sb_exw[:, t * 15: t * 15 + 15],
                    sb_ssr[:, t: t + 1]))
            cw_ev = {}
            trans_mm = {}
            for tt in range(16):
                ve(lambda e, t=tt: e.tensor_copy(sb_cwp[:, 0:15],
                                                 sb_cww[:, t * 15: t * 15 + 15]),
                   waits=[("s_mm", trans_mm.get(tt - 1, 0))])
                pb = PTB[0] if tt % 2 == 0 else PTB[1]
                twl = [("s_ve", em.now("s_ve")), ("s_dinit", em.now("s_dinit"))]
                if tt >= 2:
                    twl.append(("s_sc", cw_ev[tt - 2]))
                trans(pb[:, 0:128], sb_cwp[:], waits=twl)
                trans_mm[tt] = em.now("s_mm")
                cw_ev[tt] = sc(lambda e, b=pb, t=tt:
                               e.copy(sb_cw[0:15, t * 128:(t + 1) * 128], b[0:15, 0:128]),
                               waits=[("s_mm", em.now("s_mm"))])
            if DEBUG_DUMPS:
                wl = [("s_sc", em.now("s_sc")), ("s_ve", em.now("s_ve"))]
                dma("gpsimd", dumps["d_cw"], sb_cw[:], "s_dst", waits=wl)
                dma("gpsimd", dumps["d_cwp"], sb_cwp[:], "s_dst", waits=wl)
                dma("sync", dumps["d_lg"], sb_lgw[:, 15 * 15:16 * 15], "s_dst", waits=wl)
                dma("sync", dumps["d_t8"], sb_t8w[:, 15 * 8:16 * 8], "s_dst", waits=wl)

        if PHASE_LIMIT >= 4:
            em.run_phase("router", phase4)
        ph4.close()

        ph5 = ExitStack()
        sb_H = [ph5.enter_context(nc.sbuf_tensor(f"sb_H{i}", [128, NCH * TH], BF16))
                for i in range(2)]
        sb_rt = ph5.enter_context(nc.sbuf_tensor("sb_rt", [128, NCH * TH], F32))
        sb_cwr = ph5.enter_context(nc.sbuf_tensor("sb_cwr", [128, TH], BF16))
        sb_R05 = ph5.enter_context(nc.sbuf_tensor("sb_R05", [1, TH], F32))
        sb_R15 = ph5.enter_context(nc.sbuf_tensor("sb_R15", [1, TH], F32))
        sb_R25 = ph5.enter_context(nc.sbuf_tensor("sb_R25", [1, TH], F32))
        sb_mrep5 = ph5.enter_context(nc.sbuf_tensor("sb_mrep5", [128, TH], F32))
        sb_rrep5 = ph5.enter_context(nc.sbuf_tensor("sb_rrep5", [128, TH], F32))
        sb_t5 = ph5.enter_context(nc.sbuf_tensor("sb_t5", [128, TH], F32))
        sb_sq5 = ph5.enter_context(nc.sbuf_tensor("sb_sq5", [128, TH], BF16))
        sb_xb5 = ph5.enter_context(nc.sbuf_tensor("sb_xb5", [128, TH], BF16))
        sb_ob = [ph5.enter_context(nc.sbuf_tensor(f"sb_ob{i}", [128, TH], F32)) for i in range(2)]
        sb_x25 = [ph5.enter_context(nc.sbuf_tensor(f"sb_x25{i}", [128, TH], F32)) for i in range(2)]

        wslot_war = {}

        def ffn_half(half, wst):
            tok0 = half * TH

            def load_w(e_idx, which, slot):
                sem = f"s_dw{slot}"
                src = (sw1 if which == 0 else sw2) if e_idx < 0 else \
                    (ew1 if which == 0 else ew2)[e_idx]
                war = wst.get(slot, 0)
                dma("gpsimd", sb3(WB[slot], 1024), wchunk(src),
                    sem, waits=[("s_mm", war)] if war else ())
                return em.now(sem)

            cyc = [(0, 1), (2, 0), (1, 2)]
            rdy = {}
            slot_of = {}
            for e_i in range(-1, E):
                slot_of[e_i] = cyc[(e_i + 1) % 3]
            rdy[-1] = (load_w(-1, 0, slot_of[-1][0]), load_w(-1, 1, slot_of[-1][1]))

            for e_i in range(-1, E):
                s1, s2 = slot_of[e_i]
                r1, r2 = rdy[e_i]
                hbuf = sb_H[(e_i + 1) % 2]
                hbuf_war = wst.get(f"h{(e_i + 1) % 2}", 0)
                if e_i + 1 < E:
                    n1, n2 = slot_of[e_i + 1]
                    rdy1_next = load_w(e_i + 1, 0, n1)
                bs = None
                cw_rdy = 0
                if e_i >= 0:
                    bs = "s_db0" if e_i % 2 == 0 else "s_db1"
                    cw_rdy = bcast(sb_cw[e_i:e_i + 1, tok0: tok0 + TH], sb_cwr[:, :], bs,
                                   waits=[("s_ve", wst.get("cw_war", 0)),
                                          ("s_sc", wst["cw_sc"])])
                h_ev = {}
                h_rdy = {}
                for tc in range(2):
                    for hc in range(NCH):
                        k = tc * NCH + hc
                        ps = PS[0] if k % 2 == 0 else PS[1]
                        wl = [(f"s_dw{s1}", r1), ("s_mm", hbuf_war)]
                        if k >= 2:
                            wl.append(("s_sc", h_ev[k - 2]))
                        for dc in range(NCH):
                            mme(ps[:], WB[s1][:, dc * 1024 + hc * 128: dc * 1024 + hc * 128 + 128],
                                sb_lnx[:, dc * T + tok0 + tc * 512: dc * T + tok0 + tc * 512 + 512],
                                start=(dc == 0), stop=(dc == NCH - 1),
                                waits=wl if dc == 0 else (), inc=(dc == NCH - 1))
                        bias = sb_sb1[:, hc:hc + 1] if e_i < 0 else \
                            sb_eb1[:, e_i * NCH + hc: e_i * NCH + hc + 1]
                        h_ev[k] = sc(
                            lambda e, pp=ps, h=hc, t=tc, b=bias, hb=hbuf:
                            e.activation(out=hb[:, h * TH + t * 512: h * TH + t * 512 + 512],
                                         in_=pp[:], func=ACT.Relu, bias=b),
                            waits=[("s_mm", em.now("s_mm"))])
                    if e_i >= 0:
                        hb3 = hbuf[:].rearrange("p (h t) -> p h t", t=TH)
                        cw3 = sb_cwr[:].rearrange("p (o t) -> p o t", o=1)
                        ve(lambda e, h3=hb3, c3=cw3, t=tc:
                           e.tensor_mul(h3[:, :, t * 512:(t + 1) * 512],
                                        h3[:, :, t * 512:(t + 1) * 512],
                                        c3[:, :, t * 512:(t + 1) * 512]
                                        .to_broadcast([128, NCH, 512])),
                           waits=[("s_sc", h_ev[tc * NCH + NCH - 1]), (bs, cw_rdy)])
                        h_rdy[tc] = ("s_ve", em.now("s_ve"))
                    else:
                        h_rdy[tc] = ("s_sc", h_ev[tc * NCH + NCH - 1])
                if e_i >= 0:
                    wst["cw_war"] = em.now("s_ve")
                wst[s1] = em.now("s_mm")
                if e_i + 1 < E:
                    rdy[e_i + 1] = (rdy1_next, load_w(e_i + 1, 1, n2))
                w2_ev = {}
                for tc in range(2):
                    for oc in range(NCH):
                        k = tc * NCH + oc
                        psap = W2P[k % 2]
                        wl = [(f"s_dw{s2}", r2), h_rdy[tc]]
                        if k >= 2:
                            wl.append(w2_ev[k - 2])
                        for hc in range(NCH):
                            mme(psap, WB[s2][:, hc * 1024 + oc * 128: hc * 1024 + oc * 128 + 128],
                                hbuf[:, hc * TH + tc * 512: hc * TH + tc * 512 + 512],
                                start=(hc == 0), stop=(hc == NCH - 1 and e_i >= 0),
                                waits=wl if hc == 0 else (), inc=(hc == NCH - 1 and e_i >= 0))
                        rt_sl = sb_rt[:, oc * TH + tc * 512: oc * TH + tc * 512 + 512]
                        if e_i < 0:
                            mme(psap, sb_eb2[0:15, oc * 128: oc * 128 + 128],
                                sb_cw[0:15, tok0 + tc * 512: tok0 + tc * 512 + 512],
                                start=False, stop=True,
                                waits=[("s_dcw1", em.now("s_dcw1"))], inc=True)
                            w2_ev[k] = ("s_sc", sc(
                                lambda e, pp=psap, o=oc, r=rt_sl:
                                e.add(r, pp, sb_sb2[:, o:o + 1]),
                                waits=[("s_mm", em.now("s_mm")),
                                       ("s_ve", wst.get("rt_read", 0))]))
                        else:
                            w2_ev[k] = ("s_ve", ve(
                                lambda e, pp=psap, r=rt_sl:
                                e.tensor_add(r, pp, r),
                                waits=[("s_mm", em.now("s_mm")),
                                       ("s_sc", em.now("s_sc")),
                                       ("s_ve", wst.get("rt_war", 0))]))
                wst[s2] = em.now("s_mm")
                wst[f"h{(e_i + 1) % 2}"] = em.now("s_mm")

            acc_ve = em.now("s_ve")
            stat5_mm = {}
            for c in range(NCH):
                wl = [("s_ve", acc_ve), ("s_mm", em.now("s_mm"))] if c == 0 else \
                    [("s_mm", stat5_mm.get(c - 1, 0))]
                sc(lambda e, cc=c: e.copy(sb_xb5[:], sb_rt[:, cc * TH:(cc + 1) * TH]),
                   waits=wl)
                sc(lambda e: e.activation(out=sb_sq5[:], in_=sb_xb5[:], func=ACT.Square))
                scn = em.now("s_sc")
                for qc in range(2):
                    ps = PS[6] if qc == 0 else PS[7]
                    mme(ps[0:1, :], sb_ones[:], sb_xb5[:, qc * 512:(qc + 1) * 512],
                        start=(c == 0), stop=(c == NCH - 1),
                        waits=[("s_sc", scn), ("s_ve", wst.get("ln_war", 0))],
                        inc=(c == NCH - 1))
                    pq = SUMAP[2] if qc == 0 else SUMAP[3]
                    mme(pq, sb_ones[:], sb_sq5[:, qc * 512:(qc + 1) * 512],
                        start=(c == 0), stop=(c == NCH - 1),
                        inc=(c == NCH - 1 or qc == 1))
                stat5_mm[c] = em.now("s_mm")
            ln_rows(sb_R05, sb_R15, sb_R25, TH,
                    [(PS[6][0:1, :], 0, 512), (PS[7][0:1, :], 512, 512)],
                    [(SUMAP[2], 0, 512), (SUMAP[3], 512, 512)],
                    sb_mrep5, sb_rrep5, "s_db2", "s_db3")
            wst["ln_war"] = em.now("s_ve")
            st_done = {}
            out_tt = {}
            for c in range(NCH):
                sl = c % 2
                sx = "s_dxr0" if sl == 0 else "s_dxr1"
                rwl = [("s_ve", out_tt.get(c - 2, 0))] if c >= 2 else \
                    [("s_dx2w", em.now("s_dx2w")), ("s_ve", em.now("s_ve")),
                     ("s_dst", em.now("s_dst"))]
                dma("sync", sb_x25[sl][:], x2s[c, :, tok0: tok0 + TH], sx, waits=rwl)
                wl = [(sx, em.now(sx)), ("s_db2", em.now("s_db2")),
                      ("s_db3", em.now("s_db3"))]
                if c >= 2:
                    wl.append(("s_dst", st_done[c - 2]))
                ve(lambda e, cc=c: e.tensor_sub(sb_t5[:], sb_rt[:, cc * TH:(cc + 1) * TH],
                                                sb_mrep5[:]), waits=wl)
                ve(lambda e: e.tensor_mul(sb_t5[:], sb_t5[:], sb_rrep5[:]))
                out_tt[c] = ve(lambda e, s=sl: e.tensor_add(sb_ob[s][:], sb_t5[:],
                                                            sb_x25[s][:]))
                st_done[c] = dma("sync", out_h[c, :, tok0: tok0 + TH], sb_ob[sl][:],
                                 "s_dst", waits=[("s_ve", em.now("s_ve"))])
            wst["rt_war"] = em.now("s_ve")
            wst["rt_read"] = em.now("s_ve")

        def phase5():
            em.guard_all()
            wst = {i: em.now("s_mm") for i in range(3)}
            wst["h0"] = em.now("s_mm")
            wst["h1"] = em.now("s_mm")
            wst["rt_read"] = 0
            wst["rt_war"] = em.now("s_ve")
            wst["cw_sc"] = em.now("s_sc")
            ffn_half(0, wst)
            ffn_half(1, wst)
            if em.act("sync", [("s_dst", em.now("s_dst"))]):
                pass

        if PHASE_LIMIT >= 5:
            em.run_phase("ffn", phase5)
        ph5.close()

        def drain():
            em.guard_all()

        em.run_phase("drain", drain)

    return nc


def prep_inputs(inputs):
    f32 = np.float32
    x = np.asarray(inputs["L0_attn_x"], dtype=f32)
    tab = np.asarray(inputs["L0_attn_rel_pos_table"], dtype=f32)

    def cb(b):
        return np.ascontiguousarray(np.asarray(b, dtype=f32).reshape(NCH, 128).T)

    shared = {
        "wq": np.ascontiguousarray(np.asarray(inputs["L0_attn_WQ"], dtype=f32).T),
        "wk": np.ascontiguousarray(np.asarray(inputs["L0_attn_WK"], dtype=f32).T),
        "wv": np.ascontiguousarray(np.asarray(inputs["L0_attn_WV"], dtype=f32).T),
        "wo": np.ascontiguousarray(np.asarray(inputs["L0_attn_WO"], dtype=f32).T),
        "sw1": np.ascontiguousarray(np.asarray(inputs["L0_ffn_sW1"], dtype=f32)),
        "sw2": np.ascontiguousarray(np.asarray(inputs["L0_ffn_sW2"], dtype=f32)),
        "ew1": np.ascontiguousarray(np.asarray(inputs["L0_ffn_eW1"], dtype=f32)),
        "ew2": np.ascontiguousarray(np.asarray(inputs["L0_ffn_eW2"], dtype=f32)),
        "rw": np.ascontiguousarray(np.asarray(inputs["L0_ffn_router_W"], dtype=f32)),
        "bqc": cb(inputs["L0_attn_bQ"]), "bkc": cb(inputs["L0_attn_bK"]),
        "bvc": cb(inputs["L0_attn_bV"]), "boc": cb(inputs["L0_attn_bO"]),
        "sb1c": cb(inputs["L0_ffn_sb1"]), "sb2c": cb(inputs["L0_ffn_sb2"]),
        "eb1r": np.ascontiguousarray(
            np.asarray(inputs["L0_ffn_eb1"], dtype=f32).reshape(E, NCH, 128)
            .transpose(2, 0, 1).reshape(128, E * NCH)),
        "eb2n": np.ascontiguousarray(np.asarray(inputs["L0_ffn_eb2"], dtype=f32)),
        "rbrep": np.ascontiguousarray(np.broadcast_to(
            np.asarray(inputs["L0_ffn_router_b"], dtype=f32)[None, :], (128, E))),
        "onesb": np.ones((128, 1), dtype=ml_dtypes.bfloat16),
        "indb": np.ascontiguousarray(np.concatenate(
            [np.tile(np.array([[1, 0]], f32), (64, 1)),
             np.tile(np.array([[0, 1]], f32), (64, 1))], axis=0)
            .astype(ml_dtypes.bfloat16)),
        "identb": np.eye(128, dtype=ml_dtypes.bfloat16),
        "id64db": np.ascontiguousarray(np.concatenate(
            [np.eye(64), np.eye(64)], axis=0).astype(ml_dtypes.bfloat16)),
    }
    pidx = np.arange(128)[:, None]
    midx = np.arange(GW)[None, :]
    idx = 1663 + midx - pidx
    g = tab[idx, :]
    g = np.ascontiguousarray(g.transpose(2, 0, 1))
    g[:, idx < 2047] = -1e30
    shared["g"] = g

    per_core = []
    for c in range(8):
        d = dict(shared)
        d["xT"] = np.ascontiguousarray(x[c].T)
        per_core.append(d)
    return per_core


def kernel(**inputs):
    in_maps = prep_inputs(inputs)
    nc = build_program()
    res = run_bass_kernel_spmd(nc, in_maps, core_ids=list(range(8)))
    outs = []
    for c in range(8):
        o = res.results[c]["out"].reshape(D, T)
        outs.append(np.ascontiguousarray(o.T))
    return np.stack(outs, axis=0).astype(np.float32)


# version 21
# speedup vs baseline: 1.0169x; 1.0088x over previous
import numpy as np
import ml_dtypes
from contextlib import ExitStack

import concourse.bass as bass
import concourse.mybir as mybir
from concourse.bass_utils import run_bass_kernel_spmd

F32 = mybir.dt.float32
BF16 = mybir.dt.bfloat16
AX = mybir.AxisListType.X
ALU = mybir.AluOpType
ACT = mybir.ActivationFunctionType

D = 1024
T = 2048
NCH = 8
NH = 16
E = 15
QB = 512
NQB = T // QB
TH = 1024
GW = 2432
EPS = 1e-5

DEBUG_DUMPS = False
PHASE_LIMIT = 5

ENGS = ["sync", "gpsimd", "tensor", "vector", "scalar"]
COMPUTE_OWNER = {"s_mm": "tensor", "s_ve": "vector", "s_sc": "scalar"}


class Em:
    def __init__(self, nc, ctx, semnames):
        self.nc = nc
        self.sems = {n: ctx.enter_context(nc.semaphore(n)) for n in semnames}
        self.count = {n: 0 for n in semnames}
        self.hw = {e: {} for e in ENGS}
        self.cur = None
        self.eng = None

    def now(self, sem):
        return self.count[sem]

    def act(self, eng, waits=()):
        on = self.cur == eng
        for (s, v) in waits:
            if v <= 0 or COMPUTE_OWNER.get(s) == eng:
                continue
            if on and v > self.hw[eng].get(s, 0):
                self.eng.wait_ge(self.sems[s], v)
                self.hw[eng][s] = v
        return on

    def fin(self, ins, sem, amt):
        self.count[sem] += amt
        if ins is not None:
            ins.then_inc(self.sems[sem], amt)
        return self.count[sem]

    def guard_all(self):
        for n in self.sems:
            v = self.count[n]
            if v <= 0 or COMPUTE_OWNER.get(n) == self.cur:
                continue
            if v > self.hw[self.cur].get(n, 0):
                self.eng.wait_ge(self.sems[n], v)
                self.hw[self.cur][n] = v

    def run_phase(self, name, fn):
        snap = dict(self.count)
        hw_snap = {e: dict(self.hw[e]) for e in ENGS}
        final = [None]
        with self.nc.Block(name) as blk:
            for engname in ENGS:
                self.count = dict(snap)
                self.hw[engname] = dict(hw_snap[engname])
                self.cur = engname

                def body(eng, _fn=fn):
                    self.eng = eng
                    _fn()
                    final[0] = dict(self.count)

                getattr(blk, engname)(body)
        self.count = final[0]
        self.cur = None


def build_program():
    nc = bass.Bass("TRN2", target_bir_lowering=False, debug=False)

    def din(name, shape, dtype=F32):
        return nc.dram_tensor(name, shape, dtype, kind="ExternalInput").ap()

    xT = din("xT", [D, T])
    wq = din("wq", [D, D]); wk = din("wk", [D, D])
    wv = din("wv", [D, D]); wo = din("wo", [D, D])
    sw1 = din("sw1", [D, D]); sw2 = din("sw2", [D, D])
    ew1 = din("ew1", [E, D, D]); ew2 = din("ew2", [E, D, D])
    rw = din("rw", [D, E])
    gH = din("g", [NH, 128, GW])
    bqc = din("bqc", [128, NCH]); bkc = din("bkc", [128, NCH])
    bvc = din("bvc", [128, NCH]); boc = din("boc", [128, NCH])
    sb1c = din("sb1c", [128, NCH]); sb2c = din("sb2c", [128, NCH])
    eb1r = din("eb1r", [128, E * NCH]); eb2n = din("eb2n", [E, D])
    rbrep = din("rbrep", [128, E])
    onesb = din("onesb", [128, 1], BF16)
    indb = din("indb", [128, 2], BF16)
    identb = din("identb", [128, 128], BF16)
    id64db = din("id64db", [128, 64], BF16)

    out_h = nc.dram_tensor("out", [NCH, 128, T], F32, kind="ExternalOutput").ap()
    _dk = "ExternalOutput" if DEBUG_DUMPS else "Internal"
    x2s = nc.dram_tensor("x2s", [NCH, 128, T], F32, kind=_dk).ap()
    bscr = nc.dram_tensor("bscr", [4, T], F32, kind="Internal").ap()
    bscrb = nc.dram_tensor("bscrb", [4, T], BF16, kind="Internal").ap()
    oOs = nc.dram_tensor("oOs", [NCH, 128, T], BF16, kind=_dk).ap()

    dumps = {}
    if DEBUG_DUMPS:
        for name, shape in [
            ("d_ln1", [128, NCH * T]), ("d_q", [128, T]), ("d_k", [128, T]),
            ("d_va", [128, 16 * 130]), ("d_pt0", [128, 16 * QB]),
            ("d_attno", [128, NCH * T]), ("d_ln2", [128, NCH * T]),
            ("d_cw", [16, T]), ("d_negc", [128, 2]),
            ("d_m3", [1, T]), ("d_r3", [1, T]), ("d_m3b", [1, T]), ("d_r3b", [1, T]),
            ("d_R0", [1, T]), ("d_R1", [1, T]), ("d_R2", [1, T]),
            ("d_cwp", [128, 128]), ("d_lg", [128, 15]), ("d_t8", [128, 8]),
            ("d_mr1", [128, T]), ("d_rr1", [128, T]),
        ]:
            dumps[name] = nc.dram_tensor(name, shape, F32, kind="ExternalOutput").ap()

    semnames = (
        ["s_mm", "s_ve", "s_sc"]
        + ["s_dinit", "s_dw0", "s_dw1", "s_dw2"]
        + ["s_dg0", "s_dg1", "s_dg2", "s_dtb0", "s_dtb1"]
        + ["s_db0", "s_db1", "s_db2", "s_db3", "s_dbw"]
        + ["s_dxr0", "s_dxr1", "s_dor0", "s_dor1", "s_dx2w", "s_doow",
           "s_dst", "s_dcw0", "s_dcw1", "s_dxc0", "s_dxc1"]
    )

    with ExitStack() as ctx:
        em = Em(nc, ctx, semnames)
        alloc = ctx.enter_context

        sb_bq = alloc(nc.sbuf_tensor("sb_bq", [128, NCH], F32))
        sb_bk = alloc(nc.sbuf_tensor("sb_bk", [128, NCH], F32))
        sb_bv = alloc(nc.sbuf_tensor("sb_bv", [128, NCH], F32))
        sb_bo = alloc(nc.sbuf_tensor("sb_bo", [128, NCH], F32))
        sb_sb1 = alloc(nc.sbuf_tensor("sb_sb1", [128, NCH], F32))
        sb_sb2 = alloc(nc.sbuf_tensor("sb_sb2", [128, NCH], F32))
        sb_eb1 = alloc(nc.sbuf_tensor("sb_eb1", [128, E * NCH], F32))
        sb_rb = alloc(nc.sbuf_tensor("sb_rb", [128, E], F32))
        sb_eb2 = alloc(nc.sbuf_tensor("sb_eb2", [E, D], BF16))
        sb_rw = alloc(nc.sbuf_tensor("sb_rw", [128, NCH * E], BF16))
        sb_ones = alloc(nc.sbuf_tensor("sb_ones", [128, 1], BF16))
        sb_ind = alloc(nc.sbuf_tensor("sb_ind", [128, 2], BF16))
        sb_id = alloc(nc.sbuf_tensor("sb_id", [128, 128], BF16))
        sb_id64 = alloc(nc.sbuf_tensor("sb_id64", [128, 64], BF16))
        sb_bq8 = alloc(nc.sbuf_tensor("sb_bq8", [128, NCH], F32))
        sb_negc = alloc(nc.sbuf_tensor("sb_negc", [128, 2], F32))
        sb_cw = alloc(nc.sbuf_tensor("sb_cw", [16, T], BF16))
        sb_qm = alloc(nc.sbuf_tensor("sb_qm", [2, 16], F32))
        sb_eps = alloc(nc.sbuf_tensor("sb_eps", [1, 1], F32))

        PSA = alloc(nc.psum_tensor("psa", [128, 1024], F32))
        PSB = alloc(nc.psum_tensor("psb", [128, 1024], F32))
        PSW0 = alloc(nc.psum_tensor("psw0", [128, 1024], F32))
        PSW1 = alloc(nc.psum_tensor("psw1", [128, 1024], F32))
        PS0 = PSA[:, 0:512]
        PS1 = PSA[:, 512:1024]
        PS6 = PSB[:, 0:512]
        PS7 = PSB[:, 512:1024]
        PS = [PS0, PS1, None, None, None, None, PS6, PS7]
        PSW = [PSW0, PSW1]
        SUMAP = [PSW0[0:1, 0:512], PSW0[0:1, 512:1024],
                 PSW1[0:1, 0:512], PSW1[0:1, 512:1024]]
        SQAP = [PS0[0:1, :], PS1[0:1, :], PS6[0:1, :], PS7[0:1, :]]
        W2P = [PSW0[:, 0:512], PSW0[:, 512:1024]]
        PTB = [PSW1[:].bitcast(BF16)[:, 0:1024], PSW1[:].bitcast(BF16)[:, 1024:2048]]

        WB = [alloc(nc.sbuf_tensor(f"sb_w{i}", [128, NCH * 1024], BF16)) for i in range(3)]
        sb_lnx = alloc(nc.sbuf_tensor("sb_lnx", [128, NCH * T], BF16))

        def wchunk(ap):
            return ap.rearrange("(c p) o -> c p o", p=128).transpose([1, 0, 2])

        def sb3(t, inner):
            return t[:].rearrange("p (c o) -> p c o", o=inner)
        def dma(queue, out, in_, sem, waits=()):
            ins = None
            if em.act(queue, waits):
                ins = em.eng.dma_start(out=out, in_=in_)
            return em.fin(ins, sem, 16)

        BROW = {"s_db0": 0, "s_db1": 1, "s_db2": 2, "s_db3": 3}

        def bcast(row_ap, out_ap, sem, waits=(), queue="sync"):
            n_part = out_ap.shape[0]
            n = row_ap.shape[-1]
            row = BROW[sem]
            scr = bscr if row_ap.dtype == F32 else bscrb
            ins = None
            if em.act(queue, list(waits) + [(sem, em.now(sem))]):
                ins = em.eng.dma_start(out=scr[row:row + 1, 0:n], in_=row_ap)
            em.fin(ins, "s_dbw", 16)
            ins = None
            if em.act(queue, [("s_dbw", em.now("s_dbw"))]):
                ins = em.eng.dma_start(
                    out=out_ap, in_=scr[row:row + 1, 0:n].partition_broadcast(n_part))
            return em.fin(ins, sem, 16)

        def mme(out, lhsT, rhs, start, stop, waits=(), tp=None, inc=False):
            ins = None
            if em.act("tensor", waits):
                ins = em.eng.matmul(out, lhsT, rhs, start=start, stop=stop,
                                    tile_position=tp, skip_group_check=True)
            if inc:
                return em.fin(ins, "s_mm", 1)
            return None

        def trans(out, in_, waits=()):
            ins = None
            if em.act("tensor", waits):
                ins = em.eng.transpose(out, in_, sb_id[:])
            return em.fin(ins, "s_mm", 1)

        def ve(f, waits=()):
            ins = None
            if em.act("vector", waits):
                ins = f(em.eng)
            return em.fin(ins, "s_ve", 1)

        def sc(f, waits=()):
            ins = None
            if em.act("scalar", waits):
                ins = f(em.eng)
            return em.fin(ins, "s_sc", 1)

        def ln_rows(R0, R1, R2, n_tok, sum_srcs, sq_srcs, mrep, rrep, bc0, bc1):
            wl = [("s_mm", em.now("s_mm"))]
            for (ap_, c0, w) in sum_srcs:
                ve(lambda e, a=ap_, c=c0, ww=w: e.tensor_copy(R0[0:1, c:c + ww], a), waits=wl)
            for (ap_, c0, w) in sq_srcs:
                ve(lambda e, a=ap_, c=c0, ww=w: e.tensor_copy(R1[0:1, c:c + ww], a))
            ve(lambda e: e.tensor_scalar_mul(R0[0:1, :n_tok], R0[0:1, :n_tok], 1.0 / D))
            ve(lambda e: e.tensor_mul(R2[0:1, :n_tok], R0[0:1, :n_tok], R0[0:1, :n_tok]))
            ve(lambda e: e.scalar_tensor_tensor(R1[0:1, :n_tok], R1[0:1, :n_tok], 1.0 / D,
                                                R2[0:1, :n_tok], ALU.mult, ALU.subtract))
            sc(lambda e: e.activation(out=R1[0:1, :n_tok], in_=R1[0:1, :n_tok],
                                      func=ACT.Ln, bias=sb_eps[0:1, 0:1]),
               waits=[("s_ve", em.now("s_ve")), ("s_ve", eps_ve[0])])
            sc(lambda e: e.activation(out=R2[0:1, :n_tok], in_=R1[0:1, :n_tok],
                                      func=ACT.Exp, scale=-0.5))
            vnow = em.now("s_ve")
            snow = em.now("s_sc")
            b0 = bcast(R0[0:1, :n_tok], mrep[:, :n_tok], bc0, waits=[("s_ve", vnow)])
            b1 = bcast(R2[0:1, :n_tok], rrep[:, :n_tok], bc1, waits=[("s_sc", snow)],
                       queue="gpsimd")
            return b0, b1

        ph1 = ExitStack()
        sb_x1 = ph1.enter_context(nc.sbuf_tensor("sb_x1", [128, NCH * T], F32))
        sb_xbc = [ph1.enter_context(nc.sbuf_tensor("sb_xbc0", [128, T], BF16))] * 2
        sb_sqc = [ph1.enter_context(nc.sbuf_tensor("sb_sqc0", [128, T], BF16))] * 2
        sb_R0 = ph1.enter_context(nc.sbuf_tensor("sb_R0", [1, T], F32))
        sb_R1 = ph1.enter_context(nc.sbuf_tensor("sb_R1", [1, T], F32))
        sb_R2 = ph1.enter_context(nc.sbuf_tensor("sb_R2", [1, T], F32))
        sb_mrep = ph1.enter_context(nc.sbuf_tensor("sb_mrep", [128, T], F32))
        sb_rrep = ph1.enter_context(nc.sbuf_tensor("sb_rrep", [128, T], F32))

        xTc = xT.rearrange("(c p) t -> c p t", p=128)

        eps_ve = [0]

        def phase1():
            eps_ve[0] = ve(lambda e: e.memset(sb_eps[:], EPS))
            for (dst, src) in [(sb_bq, bqc), (sb_bk, bkc), (sb_bv, bvc), (sb_bo, boc),
                               (sb_sb1, sb1c), (sb_sb2, sb2c), (sb_eb1, eb1r),
                               (sb_rb, rbrep), (sb_ones, onesb), (sb_ind, indb),
                               (sb_id, identb), (sb_id64, id64db)]:
                dma("sync", dst[:], src, "s_dinit")
            dma("gpsimd", sb3(sb_rw, E), wchunk(rw), "s_dcw0")
            dma("gpsimd", sb_eb2[:], eb2n, "s_dcw1")
            for c in range(NCH):
                sx = "s_dxc0" if c % 2 == 0 else "s_dxc1"
                dma("sync", sb_x1[:, c * T:(c + 1) * T], xTc[c], sx)
            mm_after = {}
            for c in range(NCH):
                sl = c % 2
                sxc = "s_dxc0" if c % 2 == 0 else "s_dxc1"
                sc(lambda e, s=sl, cc=c: e.copy(sb_xbc[s][:], sb_x1[:, cc * T:(cc + 1) * T]),
                   waits=[(sxc, 16 * (c // 2 + 1)), ("s_mm", mm_after.get(c - 1, 0))])
                sc(lambda e, s=sl: e.activation(out=sb_sqc[s][:], in_=sb_xbc[s][:],
                                                func=ACT.Square))
                scn = em.now("s_sc")
                for qc in range(NQB):
                    mme(SUMAP[qc], sb_ones[:], sb_xbc[sl][:, qc * 512:(qc + 1) * 512],
                        start=(c == 0), stop=(c == NCH - 1),
                        waits=[("s_sc", scn)], inc=(c == NCH - 1))
                for qc in range(NQB):
                    mme(SQAP[qc], sb_ones[:], sb_sqc[sl][:, qc * 512:(qc + 1) * 512],
                        start=(c == 0), stop=(c == NCH - 1),
                        inc=(c == NCH - 1 or qc == NQB - 1))
                mm_after[c] = em.now("s_mm")
            ln_rows(sb_R0, sb_R1, sb_R2, T,
                    [(SUMAP[qc], qc * 512, 512) for qc in range(NQB)],
                    [(SQAP[qc], qc * 512, 512) for qc in range(NQB)],
                    sb_mrep, sb_rrep, "s_db0", "s_db1")
            for c in range(NCH):
                wl = [("s_db0", em.now("s_db0")), ("s_db1", em.now("s_db1"))]
                ve(lambda e, cc=c: e.tensor_sub(sb_x1[:, cc * T:(cc + 1) * T],
                                                sb_x1[:, cc * T:(cc + 1) * T],
                                                sb_mrep[:]), waits=wl)
                ve(lambda e, cc=c: e.tensor_mul(sb_lnx[:, cc * T:(cc + 1) * T],
                                                sb_x1[:, cc * T:(cc + 1) * T], sb_rrep[:]))
            if DEBUG_DUMPS:
                wl = [("s_ve", em.now("s_ve")), ("s_sc", em.now("s_sc")),
                      ("s_db0", em.now("s_db0")), ("s_db1", em.now("s_db1"))]
                dma("gpsimd", dumps["d_ln1"], sb_lnx[:], "s_dst", waits=wl)
                dma("sync", dumps["d_R0"], sb_R0[0:1, :], "s_dst", waits=wl)
                dma("sync", dumps["d_R1"], sb_R1[0:1, :], "s_dst")
                dma("sync", dumps["d_R2"], sb_R2[0:1, :], "s_dst")
                dma("sync", dumps["d_mr1"], sb_mrep[:], "s_dst")
                dma("sync", dumps["d_rr1"], sb_rrep[:], "s_dst")

        em.run_phase("ln1", phase1)
        ph1.close()

        phA = ExitStack()
        sb_ao = phA.enter_context(nc.sbuf_tensor("sb_ao", [128, NCH * T], BF16))
        ph2 = ExitStack()
        sb_g = [ph2.enter_context(nc.sbuf_tensor(f"sb_g{i}", [128, 2 * GW], BF16)) for i in range(2)]
        sb_q = ph2.enter_context(nc.sbuf_tensor("sb_q", [128, T], BF16))
        sb_k = ph2.enter_context(nc.sbuf_tensor("sb_k", [128, T], BF16))
        sb_vf = ph2.enter_context(nc.sbuf_tensor("sb_vf", [128, T], BF16))
        sb_va = ph2.enter_context(nc.sbuf_tensor("sb_va", [128, 16 * 130], BF16))
        sb_pt2 = ph2.enter_context(nc.sbuf_tensor("sb_pt2", [128, 16 * 2 * QB], BF16))
        sb_rr = ph2.enter_context(nc.sbuf_tensor("sb_rr", [1, 2 * QB], F32))
        sb_cc = ph2.enter_context(nc.sbuf_tensor("sb_cc", [2, 16], F32))
        AVBANK = {0: 0, 1: 1, 2: 0, 3: 0}


        def attention():
            em.guard_all()
            st = {
                "pair_mm": em.now("s_mm"),
                "pair_ve": em.now("s_ve"),
                "pair_sc": em.now("s_sc"),
                "g_rdy": {},
                "g_war": {},
                "qb_norm_ve": {},
                "av_done_mm": 0,
                "tt_done": {},
            }
            for i, w in enumerate([wq, wk, wv]):
                dma("gpsimd", sb3(WB[i], 1024), wchunk(w), f"s_dw{i}")
            sc(lambda e: e.mul(WB[0][:], WB[0][:], 0.125), waits=[("s_dw0", 16)])
            sc(lambda e: e.mul(sb_bq8[:], sb_bq[:], 0.125),
               waits=[("s_dinit", em.now("s_dinit"))])
            st["wqs"] = em.now("s_sc")
            va3 = sb_va[:].rearrange("p (t c) -> p t c", c=130)
            ve(lambda e: e.memset(va3[:, :, 64:65], 1.0))
            ve(lambda e: e.memset(va3[:, :, 129:130], 1.0))
            st["va_ones"] = em.now("s_ve")
            ve(lambda e: e.memset(sb_e1[0:1, :], 1.0))
            st["e1_ve"] = em.now("s_ve")
            st["g_dma"] = {}

            def load_g(p_, war_ve):
                slot = p_ % 2
                dma("gpsimd", sb_g[slot][:].rearrange("p (h w) -> p h w", w=GW),
                    gH[2 * p_: 2 * p_ + 2].transpose([1, 0, 2]), f"s_dg{slot}",
                    waits=[("s_ve", war_ve)])
                st["g_dma"][slot] = em.now(f"s_dg{slot}")

            def exp_g(p_):
                slot = p_ % 2
                sc(lambda e, sl=slot: e.activation(out=sb_g[sl][:], in_=sb_g[sl][:],
                                                   func=ACT.Exp),
                   waits=[(f"s_dg{slot}", st["g_dma"][slot])])
                st["g_rdy"][slot] = em.now("s_sc")

            load_g(0, 0)
            exp_g(0)
            st["load_g"] = load_g
            st["exp_g"] = exp_g
            for p in range(NCH):
                pair_prologue(p, st)
                av_prev = None
                for qb in range(NQB):
                    av_prev = qb_scores(p, qb, st, av_prev)
                qb_av_tail(p, NQB - 1, st)
                ve(lambda e, pp=p: e.tensor_scalar_add(
                    sb_ao[:, pp * T:(pp + 1) * T], sb_ao[:, pp * T:(pp + 1) * T],
                    sb_bv[:, pp:pp + 1]))
                st["pair_mm"] = em.now("s_mm")
                st["pair_ve"] = em.now("s_ve")
                st["pair_sc"] = em.now("s_sc")
                st["g_war"][p % 2] = em.now("s_ve")
            if DEBUG_DUMPS:
                wl = [("s_sc", em.now("s_sc")), ("s_ve", em.now("s_ve"))]
                dma("gpsimd", dumps["d_attno"], sb_ao[:], "s_dst", waits=wl)
                dma("gpsimd", dumps["d_q"], sb_q[:], "s_dst", waits=wl)
                dma("gpsimd", dumps["d_k"], sb_k[:], "s_dst", waits=wl)
                dma("gpsimd", dumps["d_va"], sb_va[:], "s_dst", waits=wl)
                dma("gpsimd", dumps["d_pt0"], sb_pt2[:, 0:16 * QB], "s_dst", waits=wl)
                dma("sync", dumps["d_negc"], sb_negc[:], "s_dst", waits=wl)

        def pair_prologue(p, st):
            pv, pm, psc = st["pair_ve"], st["pair_mm"], st["pair_sc"]
            ev = {}
            for wi, (wsem, bias, dest) in enumerate([("s_dw0", sb_bq8, sb_q),
                                                     ("s_dw1", sb_bk, sb_k)]):
                for qb in range(NQB):
                    ps = PS[6] if qb % 2 == 0 else PS[7]
                    wl = [(wsem, 16), ("s_ve", st["qb_norm_ve"].get(1, 0)),
                          ("s_sc", st["wqs"])]
                    key = (wi, qb)
                    if (wi, qb) not in [(0, 0), (0, 1)]:
                        pk = (wi, qb - 2) if qb >= 2 else (wi - 1, qb + 2)
                        if pk in ev:
                            wl.append(("s_ve", ev[pk]))
                    for dc in range(NCH):
                        mme(ps[:], WB[wi][:, dc * 1024 + p * 128: dc * 1024 + p * 128 + 128],
                            sb_lnx[:, dc * T + qb * 512: dc * T + qb * 512 + 512],
                            start=(dc == 0), stop=(dc == NCH - 1),
                            waits=wl if dc == 0 else (), inc=(dc == NCH - 1))
                    ev[key] = ve(lambda e, pp=ps, d=dest, q=qb, b=bias:
                                 e.tensor_scalar_add(d[:, q * 512:(q + 1) * 512],
                                                     pp[:], b[:, p:p + 1]),
                                 waits=[("s_mm", em.now("s_mm")), ("s_mm", pm)])
            evv = {}
            for qb in range(NQB):
                ps = PS[6] if qb % 2 == 0 else PS[7]
                wl = [("s_dw2", 16)]
                if qb >= 2:
                    wl.append(("s_ve", evv[qb - 2]))
                else:
                    wl.append(("s_ve", ev[(1, qb + 2)]))
                for dc in range(NCH):
                    mme(ps[:], WB[2][:, dc * 1024 + p * 128: dc * 1024 + p * 128 + 128],
                        sb_lnx[:, dc * T + qb * 512: dc * T + qb * 512 + 512],
                        start=(dc == 0), stop=(dc == NCH - 1),
                        waits=wl if dc == 0 else (), inc=(dc == NCH - 1))
                evv[qb] = ve(lambda e, pp=ps, q=qb:
                             e.tensor_copy(sb_vf[:, q * 512:(q + 1) * 512], pp[:]),
                             waits=[("s_mm", em.now("s_mm")), ("s_mm", pm)])
            tev = {}
            for tc in range(16):
                pb = PTB[tc % 2]
                wl = [("s_ve", evv[tc // 4]), ("s_sc", psc)]
                if tc >= 2:
                    wl.append(tev[tc - 2])
                trans(pb[:, 0:128], sb_vf[:, tc * 128:(tc + 1) * 128], waits=wl)
                ewl = [("s_mm", em.now("s_mm")), ("s_mm", pm),
                       ("s_ve", st["va_ones"])]
                ve(lambda e, b=pb, t=tc:
                   e.tensor_copy(sb_va[:, t * 130: t * 130 + 64], b[:, 0:64]),
                   waits=ewl)
                tev[tc] = ve(lambda e, b=pb, t=tc:
                             e.tensor_copy(sb_va[:, t * 130 + 65: t * 130 + 129],
                                           b[:, 64:128]))
                tev[tc] = ("s_ve", tev[tc])
            tr_done = em.now("s_mm")
            ncm = {}
            rdone = {}
            rrg = [PS[6][0:1, :], PS[7][0:1, :], SUMAP[0], SUMAP[1]]
            for half in range(2):
                tt_cnt = {}
                for k in range(4):
                    j = half * 4 + k
                    i, qc = divmod(j, NQB)
                    src = sb_q if i == 0 else sb_k
                    sq_sl = sb_vf[:, k * 512: k * 512 + 512]
                    swl = [("s_mm", ncm.get(j - 4, tr_done))]
                    tt_cnt[j] = ve(lambda e, s=src, q=qc, o=sq_sl:
                                   e.tensor_mul(o, s[:, q * 512:(q + 1) * 512],
                                                s[:, q * 512:(q + 1) * 512]),
                                   waits=swl)
                for k in range(4):
                    j = half * 4 + k
                    mwl = [("s_ve", tt_cnt[j]), ("s_dinit", em.now("s_dinit")),
                           ("s_sc", psc)]
                    if j - 4 in rdone:
                        mwl.append(("s_ve", rdone[j - 4]))
                    mme(rrg[k], sb_ones[:], sb_vf[:, k * 512: k * 512 + 512],
                        start=True, stop=True, waits=mwl, inc=True)
                    ncm[j] = em.now("s_mm")
                for k in range(4):
                    j = half * 4 + k
                    rdone[j] = ve(lambda e, pp=rrg[k], col=j:
                                  e.reduce_max(sb_cc[0:1, col:col + 1], pp, axis=AX),
                                  waits=[("s_mm", ncm[j])])
            ve(lambda e: e.reduce_max(sb_cc[0:1, 8:9], sb_cc[0:1, 0:4], axis=AX))
            ve(lambda e: e.memset(sb_cc[0:1, 15:16], 0.0))
            ve(lambda e: e.reduce_max(sb_cc[0:1, 9:10], sb_cc[0:1, 4:8], axis=AX))
            sc(lambda e: e.activation(out=sb_cc[0:1, 10:11], in_=sb_cc[0:1, 8:9],
                                      func=ACT.Sqrt),
               waits=[("s_ve", em.now("s_ve"))])
            sc(lambda e: e.activation(out=sb_cc[0:1, 11:12], in_=sb_cc[0:1, 9:10],
                                      func=ACT.Sqrt))
            ve(lambda e: e.tensor_mul(sb_cc[0:1, 12:13], sb_cc[0:1, 10:11],
                                      sb_cc[0:1, 11:12]),
               waits=[("s_sc", em.now("s_sc"))])
            sc(lambda e: e.mul(sb_cc[0:1, 14:15], sb_cc[0:1, 12:13], -1.0),
               waits=[("s_ve", em.now("s_ve"))])
            par = p % 2
            wl = [("s_sc", em.now("s_sc")), ("s_sc", psc)]
            bcast(sb_cc[0:1, 14:15], sb_negc[:, par: par + 1], "s_db2", waits=wl)
            st["negc"] = em.now("s_db2")
            st["prologue_sc"] = em.now("s_sc")
            st["prologue_ve"] = em.now("s_ve")
            if p + 1 < NCH:
                load_g_fn = st["load_g"]
                load_g_fn(p + 1, st["g_war"].get((p + 1) % 2, 0))

        def qb_scores(p, qb, st, av_prev):
            par = p % 2
            nkc = 4 * qb + 4
            exp_done = {}
            pnkc = av_prev[1] if av_prev else 0
            for kc in range(nkc):
                psW = PSW[kc % 2]
                if kc - 2 in exp_done:
                    wv = exp_done[kc - 2]
                else:
                    tail = st.get("qb_exp_tail")
                    if tail is not None:
                        wv = max(tail[kc], st["prologue_sc"]) if qb == 0 \
                            else tail[kc]
                    else:
                        wv = max(st.get("qb_exp_last", 0), st["prologue_sc"])
                wl = [("s_sc", wv)]
                if qb == 0 and kc < 2:
                    wl.append(("s_ve", st["prologue_ve"]))
                goff = 512 * qb - 128 * kc + 384
                gsl = sb_g[p % 2][:].rearrange("p (h w) -> p h w", w=GW)
                mme(psW[:, 0:512], sb_k[0:64, kc * 128: kc * 128 + 128],
                    sb_q[0:64, qb * 512:(qb + 1) * 512],
                    start=True, stop=True, tp=(0, 0), waits=wl)
                mme(psW[:, 512:1024], sb_k[64:128, kc * 128: kc * 128 + 128],
                    sb_q[64:128, qb * 512:(qb + 1) * 512],
                    start=True, stop=True, tp=(64, 0), inc=True)
                if av_prev and kc < pnkc:
                    qprev, _ = av_prev
                    bank = AVBANK[qprev]
                    for h in range(2):
                        psA = PS[h] if bank == 0 else PS[6 + h]
                        awl = [("s_ve", st["tt_done_prev"][kc])]
                        if kc == 0:
                            awl.append(("s_ve", st["qb_norm_ve"].get(bank, 0)))
                        mme(psA[0:65, :],
                            sb_va[:, kc * 130 + 65 * h: kc * 130 + 65 * h + 65],
                            sb_pt2[:, kc * 1024 + h * 512: kc * 1024 + h * 512 + 512],
                            start=(kc == 0), stop=(kc == pnkc - 1),
                            waits=awl, inc=(kc == pnkc - 1))
                if av_prev and kc == pnkc - 1:
                    qb_norm(p, av_prev[0], st)
                mm_done = em.now("s_mm")
                ewl = [("s_mm", mm_done), ("s_db2", st["negc"]),
                       ("s_ve", st["tt_done"].get(("war", kc), 0))]
                sc(lambda e, pw=psW, k=kc:
                   e.activation(out=sb_pt2[:, k * 1024: k * 1024 + 1024],
                                in_=pw[:], func=ACT.Exp,
                                bias=sb_negc[:, par: par + 1]),
                   waits=ewl)
                exp_done[kc] = em.now("s_sc")
                pt3 = sb_pt2[:, kc * 1024: kc * 1024 + 1024].rearrange(
                    "p (h q) -> p h q", q=512)
                ve(lambda e, p3=pt3, gg=gsl, o=goff:
                   e.tensor_mul(p3, p3, gg[:, :, o: o + 512]),
                   waits=[("s_sc", exp_done[kc]),
                          (f"s_dg{p % 2}", st["g_rdy"][p % 2] if False else 0),
                          ("s_sc", st["g_rdy"][p % 2])])
                st["tt_done"][kc] = em.now("s_ve")
            st["qb_exp_last"] = em.now("s_sc")
            st["qb_exp_tail"] = (exp_done[nkc - 2], exp_done[nkc - 1])
            st["tt_done_prev"] = dict(st["tt_done"])
            return (qb, nkc)

        def qb_av_tail(p, qb, st):
            nkc = 4 * qb + 4
            bank = AVBANK[qb]
            tt_done = st["tt_done_prev"]
            for h in range(2):
                psA = PS[h] if bank == 0 else PS[6 + h]
                for kc in range(nkc):
                    wl = [("s_ve", tt_done[kc])]
                    if kc == 0:
                        wl.append(("s_ve", st["qb_norm_ve"].get(bank, 0)))
                    mme(psA[0:65, :],
                        sb_va[:, kc * 130 + 65 * h: kc * 130 + 65 * h + 65],
                        sb_pt2[:, kc * 1024 + h * 512: kc * 1024 + h * 512 + 512],
                        start=(kc == 0), stop=(kc == nkc - 1),
                        waits=wl, inc=(kc == nkc - 1))
            if p + 1 < NCH:
                st["exp_g"](p + 1)
            qb_norm(p, qb, st)

        def qb_norm(p, qb, st):
            bank = AVBANK[qb]
            PSab = PSA if bank == 0 else PSB
            av_mm = em.now("s_mm")
            buf = qb % 2
            rbuf = sb_rr[0:1, :] if buf == 0 else sb_rr2[0:1, :]
            sc(lambda e, pa=PSab: e.activation(out=sb_rln[0:1, :], in_=pa[64:65, 0:1024],
                                               func=ACT.Ln),
               waits=[("s_mm", av_mm)])
            sc(lambda e, rb=rbuf: e.activation(out=rb, in_=sb_rln[0:1, :],
                                               func=ACT.Exp, scale=-1.0),
               waits=[("s_dbw", st.get("rrw", 0))])
            bs = "s_db0" if buf == 0 else "s_db1"
            bcast(rbuf, sb_r64c[buf][:, :], bs, waits=[("s_sc", em.now("s_sc"))])
            st["rrw"] = em.now("s_dbw")
            bn = em.now(bs)
            for h in range(2):
                dsl = sb_ao[64 * h: 64 * h + 64, p * T + qb * 512: p * T + qb * 512 + 512]
                ve(lambda e, pa=PSab, d=dsl, hh=h, bf=buf:
                   e.tensor_mul(d, pa[0:64, hh * 512:(hh + 1) * 512],
                                sb_r64c[bf][:, hh * 512:(hh + 1) * 512]),
                   waits=[(bs, bn)])
            st["qb_norm_ve"][bank] = em.now("s_ve")

        sb_rr2 = ph2.enter_context(nc.sbuf_tensor("sb_rr2", [1, 2 * QB], F32))
        sb_rln = ph2.enter_context(nc.sbuf_tensor("sb_rln", [1, 2 * QB], F32))
        sb_e1 = ph2.enter_context(nc.sbuf_tensor("sb_e1", [1, 64], F32))
        sb_r64c = [ph2.enter_context(nc.sbuf_tensor(f"sb_r64c{i}", [64, 2 * QB], F32))
                   for i in range(2)]

        if PHASE_LIMIT >= 2:
            em.run_phase("attn", attention)
        ph2.close()

        ph3 = ExitStack()
        sb_R0b = ph3.enter_context(nc.sbuf_tensor("sb_R0b", [1, T], F32))
        sb_R1b = ph3.enter_context(nc.sbuf_tensor("sb_R1b", [1, T], F32))
        sb_R2b = ph3.enter_context(nc.sbuf_tensor("sb_R2b", [1, T], F32))
        sb_mrep3 = ph3.enter_context(nc.sbuf_tensor("sb_mrep3", [128, T], F32))
        sb_rrep3 = ph3.enter_context(nc.sbuf_tensor("sb_rrep3", [128, T], F32))
        sb_x3 = [ph3.enter_context(nc.sbuf_tensor(f"sb_x3{i}", [128, T], F32)) for i in range(2)]
        sb_x23 = [ph3.enter_context(nc.sbuf_tensor(f"sb_x23{i}", [128, T], F32)) for i in range(2)]
        sb_sq3 = ph3.enter_context(nc.sbuf_tensor("sb_sq3", [128, T], BF16))
        sb_xb3 = ph3.enter_context(nc.sbuf_tensor("sb_xb3", [128, T], BF16))

        def phase3():
            em.guard_all()
            attn_ve = em.now("s_ve")
            dma("gpsimd", sb3(WB[0], 1024), wchunk(wo), "s_dw0",
                waits=[("s_mm", em.now("s_mm"))])
            ev = {}
            for oc in range(NCH):
                for tb in range(NQB):
                    ps = PS[0] if (oc * NQB + tb) % 2 == 0 else PS[1]
                    wl = [("s_dw0", em.now("s_dw0")), ("s_ve", attn_ve)]
                    if (oc * NQB + tb) >= 2:
                        wl.append(("s_sc", ev[(oc * NQB + tb) - 2]))
                    for fc in range(NCH):
                        mme(ps[:], WB[0][:, fc * 1024 + oc * 128: fc * 1024 + oc * 128 + 128],
                            sb_ao[:, fc * T + tb * 512: fc * T + tb * 512 + 512],
                            start=(fc == 0), stop=(fc == NCH - 1),
                            waits=wl if fc == 0 else (), inc=(fc == NCH - 1))
                    ev[oc * NQB + tb] = sc(
                        lambda e, pp=ps, o=oc, t=tb:
                        e.add(sb_lnx[:, o * T + t * 512: o * T + t * 512 + 512],
                              pp[:], sb_bo[:, o:o + 1]),
                        waits=[("s_mm", em.now("s_mm"))])

            stat_mm = {}
            for c in range(NCH):
                swl = [("s_sc", ev[c * NQB + NQB - 1]), ("s_mm", stat_mm.get(c - 1, 0))]
                sc(lambda e, cc=c: e.activation(out=sb_sq3[:],
                                                in_=sb_lnx[:, cc * T:(cc + 1) * T],
                                                func=ACT.Square), waits=swl)
                scn = em.now("s_sc")
                for qc in range(NQB):
                    wl2 = [("s_sc", scn)]
                    if c == 0:
                        wl2.append(("s_ve", em.now("s_ve")))
                    mme(SUMAP[qc], sb_ones[:],
                        sb_lnx[:, c * T + qc * 512: c * T + qc * 512 + 512],
                        start=(c == 0), stop=(c == NCH - 1),
                        waits=wl2, inc=(c == NCH - 1))
                for qc in range(NQB):
                    mme(SQAP[qc], sb_ones[:], sb_sq3[:, qc * 512:(qc + 1) * 512],
                        start=(c == 0), stop=(c == NCH - 1),
                        inc=(c == NCH - 1 or qc == NQB - 1))
                stat_mm[c] = em.now("s_mm")
            ln_rows(sb_R0b, sb_R1b, sb_R2b, T,
                    [(SUMAP[qc], qc * 512, 512) for qc in range(NQB)],
                    [(SQAP[qc], qc * 512, 512) for qc in range(NQB)],
                    sb_mrep3, sb_rrep3, "s_db0", "s_db1")
            if DEBUG_DUMPS:
                dma("sync", dumps["d_m3"], sb_R0b[0:1, :], "s_dst",
                    waits=[("s_ve", em.now("s_ve")), ("s_sc", em.now("s_sc"))])
                dma("sync", dumps["d_r3"], sb_R2b[0:1, :], "s_dst")

            x2w = {}
            stat2_mm = {}
            tt_done = {}
            for c in range(NCH):
                sl = c % 2
                sxx = "s_dxc0" if sl == 0 else "s_dxc1"
                x2sl = sb_x23[sl][:]
                dma("sync", sb_x3[sl][:], xTc[c], sxx,
                    waits=[("s_ve", tt_done.get(c - 2, 0))])
                wl = [(sxx, em.now(sxx)), ("s_db0", em.now("s_db0")),
                      ("s_db1", em.now("s_db1")), ("s_dx2w", x2w.get(c - 2, 0)),
                      ("s_mm", em.now("s_mm"))]
                ve(lambda e, cc=c, x2b=x2sl: e.tensor_sub(
                    x2b, sb_lnx[:, cc * T:(cc + 1) * T], sb_mrep3[:]), waits=wl)
                ve(lambda e, x2b=x2sl: e.tensor_mul(x2b, x2b, sb_rrep3[:]))
                ve(lambda e, s=sl, x2b=x2sl: e.tensor_add(x2b, x2b, sb_x3[s][:]))
                tt_done[c] = em.now("s_ve")
                x2w[c] = dma("gpsimd", x2s[c], x2sl, "s_dx2w",
                             waits=[("s_ve", em.now("s_ve"))])
                sc(lambda e, x2b=x2sl: e.copy(sb_xb3[:], x2b),
                   waits=[("s_ve", em.now("s_ve")), ("s_mm", stat2_mm.get(c - 1, 0))])
                sc(lambda e: e.activation(out=sb_sq3[:], in_=sb_xb3[:], func=ACT.Square))
                scn = em.now("s_sc")
                for qc in range(NQB):
                    wl2 = [("s_sc", scn)]
                    if c == 0:
                        wl2.append(("s_ve", em.now("s_ve")))
                    mme(SUMAP[qc], sb_ones[:], sb_xb3[:, qc * 512:(qc + 1) * 512],
                        start=(c == 0), stop=(c == NCH - 1),
                        waits=wl2, inc=(c == NCH - 1))
                for qc in range(NQB):
                    mme(SQAP[qc], sb_ones[:], sb_sq3[:, qc * 512:(qc + 1) * 512],
                        start=(c == 0), stop=(c == NCH - 1),
                        inc=(c == NCH - 1 or qc == NQB - 1))
                stat2_mm[c] = em.now("s_mm")
            ln_rows(sb_R0b, sb_R1b, sb_R2b, T,
                    [(SUMAP[qc], qc * 512, 512) for qc in range(NQB)],
                    [(SQAP[qc], qc * 512, 512) for qc in range(NQB)],
                    sb_mrep3, sb_rrep3, "s_db2", "s_db3")
            if DEBUG_DUMPS:
                dma("sync", dumps["d_m3b"], sb_R0b[0:1, :], "s_dst",
                    waits=[("s_ve", em.now("s_ve")), ("s_sc", em.now("s_sc"))])
                dma("sync", dumps["d_r3b"], sb_R2b[0:1, :], "s_dst")
            app_tt = {}
            for c in range(NCH):
                sl = c % 2
                sxx = "s_dxc0" if sl == 0 else "s_dxc1"
                dma("sync", sb_x3[sl][:], x2s[c], sxx,
                    waits=[("s_ve", app_tt.get(c - 2, 0)), ("s_dx2w", x2w[c])])
                wl = [(sxx, em.now(sxx)), ("s_db2", em.now("s_db2")),
                      ("s_db3", em.now("s_db3"))]
                ve(lambda e, s=sl: e.tensor_sub(sb_x23[0][:], sb_x3[s][:], sb_mrep3[:]),
                   waits=wl)
                app_tt[c] = ve(lambda e, cc=c: e.tensor_mul(
                    sb_lnx[:, cc * T:(cc + 1) * T], sb_x23[0][:], sb_rrep3[:]))
            if DEBUG_DUMPS:
                dma("gpsimd", dumps["d_ln2"], sb_lnx[:], "s_dst",
                    waits=[("s_ve", em.now("s_ve"))])

        if PHASE_LIMIT >= 3:
            em.run_phase("wo_ln2", phase3)
        ph3.close()
        phA.close()

        ph4 = ExitStack()
        sb_lgw = ph4.enter_context(nc.sbuf_tensor("sb_lgw", [128, 16 * 15], F32))
        sb_exw = ph4.enter_context(nc.sbuf_tensor("sb_exw", [128, 16 * 15], F32))
        sb_cww = ph4.enter_context(nc.sbuf_tensor("sb_cww", [128, 16 * 15], F32))
        sb_t8w = ph4.enter_context(nc.sbuf_tensor("sb_t8w", [128, 16 * 8], F32))
        sb_ssm = ph4.enter_context(nc.sbuf_tensor("sb_ssm", [128, 16], F32))
        sb_sss = ph4.enter_context(nc.sbuf_tensor("sb_sss", [128, 16], F32))
        sb_ssr = ph4.enter_context(nc.sbuf_tensor("sb_ssr", [128, 16], F32))
        sb_cwp = ph4.enter_context(nc.sbuf_tensor("sb_cwp", [128, 128], BF16))

        shpre = [None]

        def phase4():
            em.guard_all()
            mmn = em.now("s_mm")
            dma("gpsimd", sb3(WB[1], 1024), wchunk(sw1), "s_dw1",
                waits=[("s_mm", mmn)])
            dma("gpsimd", sb3(WB[2], 1024), wchunk(sw2), "s_dw2",
                waits=[("s_mm", mmn)])
            shpre[0] = (em.now("s_dw1"), em.now("s_dw2"))
            ve(lambda e: e.memset(sb_cwp[:], 0.0))
            add_ve = {}
            for tt in range(16):
                ps = PS[6] if tt % 2 == 0 else PS[7]
                wl = [("s_dcw0", em.now("s_dcw0")), ("s_ve", add_ve.get(tt - 2, 0)),
                      ("s_ve", em.now("s_ve") if tt == 0 else 0)]
                for dc in range(NCH):
                    mme(ps[0:128, 0:15],
                        sb_lnx[:, dc * T + tt * 128: dc * T + tt * 128 + 128],
                        sb_rw[:, dc * E: dc * E + E],
                        start=(dc == 0), stop=(dc == NCH - 1),
                        waits=wl if dc == 0 else (), inc=(dc == NCH - 1))
                add_ve[tt] = ve(
                    lambda e, pp=ps, t=tt: e.tensor_add(
                        sb_lgw[:, t * 15: t * 15 + 15], pp[0:128, 0:15], sb_rb[:, :]),
                    waits=[("s_mm", em.now("s_mm")), ("s_dinit", em.now("s_dinit"))])
            for tt in range(16):
                ve(lambda e, t=tt: e.max(out=sb_t8w[:, t * 8: t * 8 + 8],
                                         in_=sb_lgw[:, t * 15: t * 15 + 15]))
            lg3 = sb_lgw[:].rearrange("p (t e) -> p t e", e=15)
            ve(lambda e: e.tensor_reduce(sb_ssm[:].rearrange("p (t o) -> p t o", o=1),
                                         lg3, AX, ALU.max, negate=True))
            for tt in range(16):
                sc(lambda e, t=tt: e.activation(
                    out=sb_exw[:, t * 15: t * 15 + 15],
                    in_=sb_lgw[:, t * 15: t * 15 + 15], func=ACT.Exp,
                    bias=sb_ssm[:, t: t + 1], accum_out=sb_sss[:, t: t + 1]),
                   waits=[("s_ve", em.now("s_ve"))] if tt == 0 else ())
            ve(lambda e: e.reciprocal(sb_ssr[:], sb_sss[:]),
               waits=[("s_sc", em.now("s_sc"))])
            for tt in range(16):
                ve(lambda e, t=tt: e.tensor_single_scalar(
                    sb_lgw[:, t * 15: t * 15 + 15], sb_lgw[:, t * 15: t * 15 + 15],
                    sb_t8w[:, t * 8 + 2: t * 8 + 3], ALU.is_ge))
            ve(lambda e: e.tensor_mul(sb_exw[:], sb_exw[:], sb_lgw[:]))
            for tt in range(16):
                ve(lambda e, t=tt: e.tensor_scalar_mul(
                    sb_cww[:, t * 15: t * 15 + 15], sb_exw[:, t * 15: t * 15 + 15],
                    sb_ssr[:, t: t + 1]))
            cw_ev = {}
            trans_mm = {}
            for tt in range(16):
                ve(lambda e, t=tt: e.tensor_copy(sb_cwp[:, 0:15],
                                                 sb_cww[:, t * 15: t * 15 + 15]),
                   waits=[("s_mm", trans_mm.get(tt - 1, 0))])
                pb = PTB[0] if tt % 2 == 0 else PTB[1]
                twl = [("s_ve", em.now("s_ve")), ("s_dinit", em.now("s_dinit"))]
                if tt >= 2:
                    twl.append(("s_sc", cw_ev[tt - 2]))
                trans(pb[:, 0:128], sb_cwp[:], waits=twl)
                trans_mm[tt] = em.now("s_mm")
                cw_ev[tt] = sc(lambda e, b=pb, t=tt:
                               e.copy(sb_cw[0:15, t * 128:(t + 1) * 128], b[0:15, 0:128]),
                               waits=[("s_mm", em.now("s_mm"))])
            if DEBUG_DUMPS:
                wl = [("s_sc", em.now("s_sc")), ("s_ve", em.now("s_ve"))]
                dma("gpsimd", dumps["d_cw"], sb_cw[:], "s_dst", waits=wl)
                dma("gpsimd", dumps["d_cwp"], sb_cwp[:], "s_dst", waits=wl)
                dma("sync", dumps["d_lg"], sb_lgw[:, 15 * 15:16 * 15], "s_dst", waits=wl)
                dma("sync", dumps["d_t8"], sb_t8w[:, 15 * 8:16 * 8], "s_dst", waits=wl)

        if PHASE_LIMIT >= 4:
            em.run_phase("router", phase4)
        ph4.close()

        ph5 = ExitStack()
        sb_H = [ph5.enter_context(nc.sbuf_tensor(f"sb_H{i}", [128, NCH * TH], BF16))
                for i in range(2)]
        sb_rt = ph5.enter_context(nc.sbuf_tensor("sb_rt", [128, NCH * TH], F32))
        sb_cwr = ph5.enter_context(nc.sbuf_tensor("sb_cwr", [128, TH], BF16))
        sb_R05 = ph5.enter_context(nc.sbuf_tensor("sb_R05", [1, TH], F32))
        sb_R15 = ph5.enter_context(nc.sbuf_tensor("sb_R15", [1, TH], F32))
        sb_R25 = ph5.enter_context(nc.sbuf_tensor("sb_R25", [1, TH], F32))
        sb_mrep5 = ph5.enter_context(nc.sbuf_tensor("sb_mrep5", [128, TH], F32))
        sb_rrep5 = ph5.enter_context(nc.sbuf_tensor("sb_rrep5", [128, TH], F32))
        sb_t5 = ph5.enter_context(nc.sbuf_tensor("sb_t5", [128, TH], F32))
        sb_sq5 = ph5.enter_context(nc.sbuf_tensor("sb_sq5", [128, TH], BF16))
        sb_xb5 = ph5.enter_context(nc.sbuf_tensor("sb_xb5", [128, TH], BF16))
        sb_ob = [ph5.enter_context(nc.sbuf_tensor(f"sb_ob{i}", [128, TH], F32)) for i in range(2)]
        sb_x25 = [ph5.enter_context(nc.sbuf_tensor(f"sb_x25{i}", [128, TH], F32)) for i in range(2)]

        wslot_war = {}

        def ffn_half(half, wst):
            tok0 = half * TH

            def load_w(e_idx, which, slot):
                sem = f"s_dw{slot}"
                src = (sw1 if which == 0 else sw2) if e_idx < 0 else \
                    (ew1 if which == 0 else ew2)[e_idx]
                war = wst.get(slot, 0)
                dma("gpsimd", sb3(WB[slot], 1024), wchunk(src),
                    sem, waits=[("s_mm", war)] if war else ())
                return em.now(sem)

            cyc = [(0, 1), (2, 0), (1, 2)]
            rdy = {}
            slot_of = {}
            for e_i in range(-1, E):
                slot_of[e_i] = cyc[(e_i + 1 + 2) % 3]
            if half == 0:
                rdy[-1] = shpre[0]
            else:
                rdy[-1] = (load_w(-1, 0, slot_of[-1][0]),
                           load_w(-1, 1, slot_of[-1][1]))

            for e_i in range(-1, E):
                s1, s2 = slot_of[e_i]
                r1, r2 = rdy[e_i]
                hbuf = sb_H[(e_i + 1) % 2]
                hbuf_war = wst.get(f"h{(e_i + 1) % 2}", 0)
                if e_i + 1 < E:
                    n1, n2 = slot_of[e_i + 1]
                    rdy1_next = load_w(e_i + 1, 0, n1)
                bs = None
                cw_rdy = 0
                if e_i >= 0:
                    bs = "s_db0" if e_i % 2 == 0 else "s_db1"
                    cw_rdy = bcast(sb_cw[e_i:e_i + 1, tok0: tok0 + TH], sb_cwr[:, :], bs,
                                   waits=[("s_ve", wst.get("cw_war", 0)),
                                          ("s_sc", wst["cw_sc"])])
                h_ev = {}
                h_rdy = {}
                for tc in range(2):
                    for hc in range(NCH):
                        k = tc * NCH + hc
                        ps = PS[0] if k % 2 == 0 else PS[1]
                        wl = [(f"s_dw{s1}", r1), ("s_mm", hbuf_war)]
                        if k >= 2:
                            wl.append(("s_sc", h_ev[k - 2]))
                        for dc in range(NCH):
                            mme(ps[:], WB[s1][:, dc * 1024 + hc * 128: dc * 1024 + hc * 128 + 128],
                                sb_lnx[:, dc * T + tok0 + tc * 512: dc * T + tok0 + tc * 512 + 512],
                                start=(dc == 0), stop=(dc == NCH - 1),
                                waits=wl if dc == 0 else (), inc=(dc == NCH - 1))
                        bias = sb_sb1[:, hc:hc + 1] if e_i < 0 else \
                            sb_eb1[:, e_i * NCH + hc: e_i * NCH + hc + 1]
                        h_ev[k] = sc(
                            lambda e, pp=ps, h=hc, t=tc, b=bias, hb=hbuf:
                            e.activation(out=hb[:, h * TH + t * 512: h * TH + t * 512 + 512],
                                         in_=pp[:], func=ACT.Relu, bias=b),
                            waits=[("s_mm", em.now("s_mm"))])
                    if e_i >= 0:
                        hb3 = hbuf[:].rearrange("p (h t) -> p h t", t=TH)
                        cw3 = sb_cwr[:].rearrange("p (o t) -> p o t", o=1)
                        ve(lambda e, h3=hb3, c3=cw3, t=tc:
                           e.tensor_mul(h3[:, :, t * 512:(t + 1) * 512],
                                        h3[:, :, t * 512:(t + 1) * 512],
                                        c3[:, :, t * 512:(t + 1) * 512]
                                        .to_broadcast([128, NCH, 512])),
                           waits=[("s_sc", h_ev[tc * NCH + NCH - 1]), (bs, cw_rdy)])
                        h_rdy[tc] = ("s_ve", em.now("s_ve"))
                    else:
                        h_rdy[tc] = ("s_sc", h_ev[tc * NCH + NCH - 1])
                if e_i >= 0:
                    wst["cw_war"] = em.now("s_ve")
                wst[s1] = em.now("s_mm")
                if e_i + 1 < E:
                    rdy[e_i + 1] = (rdy1_next, load_w(e_i + 1, 1, n2))
                w2_ev = {}
                for tc in range(2):
                    for oc in range(NCH):
                        k = tc * NCH + oc
                        psap = W2P[k % 2]
                        wl = [(f"s_dw{s2}", r2), h_rdy[tc]]
                        if k >= 2:
                            wl.append(w2_ev[k - 2])
                        for hc in range(NCH):
                            mme(psap, WB[s2][:, hc * 1024 + oc * 128: hc * 1024 + oc * 128 + 128],
                                hbuf[:, hc * TH + tc * 512: hc * TH + tc * 512 + 512],
                                start=(hc == 0), stop=(hc == NCH - 1 and e_i >= 0),
                                waits=wl if hc == 0 else (), inc=(hc == NCH - 1 and e_i >= 0))
                        rt_sl = sb_rt[:, oc * TH + tc * 512: oc * TH + tc * 512 + 512]
                        if e_i < 0:
                            mme(psap, sb_eb2[0:15, oc * 128: oc * 128 + 128],
                                sb_cw[0:15, tok0 + tc * 512: tok0 + tc * 512 + 512],
                                start=False, stop=True,
                                waits=[("s_dcw1", em.now("s_dcw1"))], inc=True)
                            w2_ev[k] = ("s_sc", sc(
                                lambda e, pp=psap, o=oc, r=rt_sl:
                                e.add(r, pp, sb_sb2[:, o:o + 1]),
                                waits=[("s_mm", em.now("s_mm")),
                                       ("s_ve", wst.get("rt_read", 0))]))
                        else:
                            w2_ev[k] = ("s_ve", ve(
                                lambda e, pp=psap, r=rt_sl:
                                e.tensor_add(r, pp, r),
                                waits=[("s_mm", em.now("s_mm")),
                                       ("s_sc", em.now("s_sc")),
                                       ("s_ve", wst.get("rt_war", 0))]))
                wst[s2] = em.now("s_mm")
                wst[f"h{(e_i + 1) % 2}"] = em.now("s_mm")

            acc_ve = em.now("s_ve")
            stat5_mm = {}
            for c in range(NCH):
                wl = [("s_ve", acc_ve), ("s_mm", em.now("s_mm"))] if c == 0 else \
                    [("s_mm", stat5_mm.get(c - 1, 0))]
                sc(lambda e, cc=c: e.copy(sb_xb5[:], sb_rt[:, cc * TH:(cc + 1) * TH]),
                   waits=wl)
                sc(lambda e: e.activation(out=sb_sq5[:], in_=sb_xb5[:], func=ACT.Square))
                scn = em.now("s_sc")
                for qc in range(2):
                    ps = PS[6] if qc == 0 else PS[7]
                    mme(ps[0:1, :], sb_ones[:], sb_xb5[:, qc * 512:(qc + 1) * 512],
                        start=(c == 0), stop=(c == NCH - 1),
                        waits=[("s_sc", scn), ("s_ve", wst.get("ln_war", 0))],
                        inc=(c == NCH - 1))
                    pq = SUMAP[2] if qc == 0 else SUMAP[3]
                    mme(pq, sb_ones[:], sb_sq5[:, qc * 512:(qc + 1) * 512],
                        start=(c == 0), stop=(c == NCH - 1),
                        inc=(c == NCH - 1 or qc == 1))
                stat5_mm[c] = em.now("s_mm")
            ln_rows(sb_R05, sb_R15, sb_R25, TH,
                    [(PS[6][0:1, :], 0, 512), (PS[7][0:1, :], 512, 512)],
                    [(SUMAP[2], 0, 512), (SUMAP[3], 512, 512)],
                    sb_mrep5, sb_rrep5, "s_db2", "s_db3")
            wst["ln_war"] = em.now("s_ve")
            st_done = {}
            out_tt = {}
            for c in range(NCH):
                sl = c % 2
                sx = "s_dxr0" if sl == 0 else "s_dxr1"
                rwl = [("s_ve", out_tt.get(c - 2, 0))] if c >= 2 else \
                    [("s_dx2w", em.now("s_dx2w")), ("s_ve", em.now("s_ve")),
                     ("s_dst", em.now("s_dst"))]
                dma("sync", sb_x25[sl][:], x2s[c, :, tok0: tok0 + TH], sx, waits=rwl)
                wl = [(sx, em.now(sx)), ("s_db2", em.now("s_db2")),
                      ("s_db3", em.now("s_db3"))]
                if c >= 2:
                    wl.append(("s_dst", st_done[c - 2]))
                ve(lambda e, cc=c: e.tensor_sub(sb_t5[:], sb_rt[:, cc * TH:(cc + 1) * TH],
                                                sb_mrep5[:]), waits=wl)
                ve(lambda e: e.tensor_mul(sb_t5[:], sb_t5[:], sb_rrep5[:]))
                out_tt[c] = ve(lambda e, s=sl: e.tensor_add(sb_ob[s][:], sb_t5[:],
                                                            sb_x25[s][:]))
                st_done[c] = dma("sync", out_h[c, :, tok0: tok0 + TH], sb_ob[sl][:],
                                 "s_dst", waits=[("s_ve", em.now("s_ve"))])
            wst["rt_war"] = em.now("s_ve")
            wst["rt_read"] = em.now("s_ve")

        def phase5():
            em.guard_all()
            wst = {i: em.now("s_mm") for i in range(3)}
            wst["h0"] = em.now("s_mm")
            wst["h1"] = em.now("s_mm")
            wst["rt_read"] = 0
            wst["rt_war"] = em.now("s_ve")
            wst["cw_sc"] = em.now("s_sc")
            ffn_half(0, wst)
            ffn_half(1, wst)
            if em.act("sync", [("s_dst", em.now("s_dst"))]):
                pass

        if PHASE_LIMIT >= 5:
            em.run_phase("ffn", phase5)
        ph5.close()

        def drain():
            em.guard_all()

        em.run_phase("drain", drain)

    return nc


def prep_inputs(inputs):
    f32 = np.float32
    x = np.asarray(inputs["L0_attn_x"], dtype=f32)
    tab = np.asarray(inputs["L0_attn_rel_pos_table"], dtype=f32)

    def cb(b):
        return np.ascontiguousarray(np.asarray(b, dtype=f32).reshape(NCH, 128).T)

    shared = {
        "wq": np.ascontiguousarray(np.asarray(inputs["L0_attn_WQ"], dtype=f32).T),
        "wk": np.ascontiguousarray(np.asarray(inputs["L0_attn_WK"], dtype=f32).T),
        "wv": np.ascontiguousarray(np.asarray(inputs["L0_attn_WV"], dtype=f32).T),
        "wo": np.ascontiguousarray(np.asarray(inputs["L0_attn_WO"], dtype=f32).T),
        "sw1": np.ascontiguousarray(np.asarray(inputs["L0_ffn_sW1"], dtype=f32)),
        "sw2": np.ascontiguousarray(np.asarray(inputs["L0_ffn_sW2"], dtype=f32)),
        "ew1": np.ascontiguousarray(np.asarray(inputs["L0_ffn_eW1"], dtype=f32)),
        "ew2": np.ascontiguousarray(np.asarray(inputs["L0_ffn_eW2"], dtype=f32)),
        "rw": np.ascontiguousarray(np.asarray(inputs["L0_ffn_router_W"], dtype=f32)),
        "bqc": cb(inputs["L0_attn_bQ"]), "bkc": cb(inputs["L0_attn_bK"]),
        "bvc": cb(inputs["L0_attn_bV"]), "boc": cb(inputs["L0_attn_bO"]),
        "sb1c": cb(inputs["L0_ffn_sb1"]), "sb2c": cb(inputs["L0_ffn_sb2"]),
        "eb1r": np.ascontiguousarray(
            np.asarray(inputs["L0_ffn_eb1"], dtype=f32).reshape(E, NCH, 128)
            .transpose(2, 0, 1).reshape(128, E * NCH)),
        "eb2n": np.ascontiguousarray(np.asarray(inputs["L0_ffn_eb2"], dtype=f32)),
        "rbrep": np.ascontiguousarray(np.broadcast_to(
            np.asarray(inputs["L0_ffn_router_b"], dtype=f32)[None, :], (128, E))),
        "onesb": np.ones((128, 1), dtype=ml_dtypes.bfloat16),
        "indb": np.ascontiguousarray(np.concatenate(
            [np.tile(np.array([[1, 0]], f32), (64, 1)),
             np.tile(np.array([[0, 1]], f32), (64, 1))], axis=0)
            .astype(ml_dtypes.bfloat16)),
        "identb": np.eye(128, dtype=ml_dtypes.bfloat16),
        "id64db": np.ascontiguousarray(np.concatenate(
            [np.eye(64), np.eye(64)], axis=0).astype(ml_dtypes.bfloat16)),
    }
    pidx = np.arange(128)[:, None]
    midx = np.arange(GW)[None, :]
    idx = 1663 + midx - pidx
    g = tab[idx, :]
    g = np.ascontiguousarray(g.transpose(2, 0, 1))
    g[:, idx < 2047] = -1e30
    shared["g"] = g

    per_core = []
    for c in range(8):
        d = dict(shared)
        d["xT"] = np.ascontiguousarray(x[c].T)
        per_core.append(d)
    return per_core


def kernel(**inputs):
    in_maps = prep_inputs(inputs)
    nc = build_program()
    res = run_bass_kernel_spmd(nc, in_maps, core_ids=list(range(8)))
    outs = []
    for c in range(8):
        o = res.results[c]["out"].reshape(D, T)
        outs.append(np.ascontiguousarray(o.T))
    return np.stack(outs, axis=0).astype(np.float32)
